# version 05 (final)
# baseline (speedup 1.0000x reference)
import contextlib
import numpy as np
import ml_dtypes
import concourse.bass as bass
import concourse.mybir as mybir
from concourse.bass_utils import run_bass_kernel_spmd

F32 = mybir.dt.float32
BF16 = mybir.dt.bfloat16
FP8 = mybir.dt.float8e4
ALU = mybir.AluOpType
AF = mybir.ActivationFunctionType
AX = mybir.AxisListType.X

B, S, D, NH, HD, FF = 4, 4096, 2048, 16, 128, 8192
FC = D // 128
OWN = 2048
NJ = OWN // 128
QT = 256
NQT = OWN // QT
HFC = FF // 128
EPS = 1e-5
NEG = -1.0e9
C448 = float(np.float32(1.0) / np.float32(448.0))
EPS30 = float(np.float32(1e-30))
RSQ = float(np.float32(1.0) / np.float32(np.sqrt(np.float32(128.0))))
INV_D = float(np.float32(1.0) / np.float32(D))

ENGS = ("pe", "ve", "ac", "sy", "gp")
DEBUG_TAPS = {}


class Prog:
    def __init__(self):
        self.streams = {e: [] for e in ENGS}
        self.cnt = {}
        self.pending = {e: [] for e in ENGS}
        self.state = {}

    def emit(self, eng, fn, waits=(), inc=None, n=1, tag=""):
        w = [(s, v) for (s, v) in waits if v > 0]
        if self.pending[eng]:
            w = self.pending[eng] + w
            self.pending[eng] = []
        if inc is not None:
            self.cnt[inc] = self.cnt.get(inc, 0) + n
        self.streams[eng].append((w, fn, inc, n, tag))
        return self.cnt.get(inc, 0)

    def tok(self, sem):
        return self.cnt.get(sem, 0)

    def barrier(self):
        snap = [(s, v) for s, v in self.cnt.items() if v > 0]
        for e in ENGS:
            self.pending[e] = list(snap)

    def check(self):
        ptr = {e: 0 for e in ENGS}
        val = {s: 0 for s in self.cnt}
        progress = True
        while progress:
            progress = False
            for e in ENGS:
                while ptr[e] < len(self.streams[e]):
                    w, fn, inc, n, tag = self.streams[e][ptr[e]]
                    if all(val.get(s, 0) >= v for (s, v) in w):
                        if inc is not None:
                            val[inc] = val.get(inc, 0) + n
                        ptr[e] += 1
                        progress = True
                    else:
                        break
        stuck = {e: ptr[e] for e in ENGS if ptr[e] < len(self.streams[e])}
        if stuck:
            det = []
            for e, i in stuck.items():
                w, fn, inc, n, tag = self.streams[e][i]
                bad = [(s, v, val.get(s, 0)) for (s, v) in w
                       if val.get(s, 0) < v]
                det.append(f"{e}[{i}/{len(self.streams[e])}] tag={tag} "
                           f"needs={bad}")
            raise RuntimeError("schedule deadlock: " + "; ".join(det))
        big = {s: v for s, v in val.items() if v > 60000}
        if big:
            raise RuntimeError(f"semaphore totals too large: {big}")


def _ap(t, off, dims):
    return bass.AP(t, off, [[t.shape[1], 128]] + [list(d) for d in dims])


def build_nc():
    nc = bass.Bass("TRN2", target_bir_lowering=False)
    T = {}
    T["xb"] = nc.dram_tensor("xb", [S, D], F32, kind="ExternalInput")
    for n, sh in (("wq", [D, D]), ("wk", [D, D]), ("wv", [D, D]),
                  ("wo", [D, D]), ("gw", [FF, D]), ("uw", [FF, D]),
                  ("dw", [D, FF]), ("lng", [1, D]), ("lnb", [1, D]),
                  ("bo2", [1, D]), ("bq16", [128, FC]), ("bk16", [128, FC]),
                  ("bv16", [128, FC]), ("mask4", [128, 4 * QT])):
        T[n] = nc.dram_tensor(n, sh, F32, kind="ExternalInput")
    T["onescol"] = nc.dram_tensor("onescol", [128, 1], BF16,
                                  kind="ExternalInput")
    T["onesrow"] = nc.dram_tensor("onesrow", [1, 128], BF16,
                                  kind="ExternalInput")
    T["y"] = nc.dram_tensor("y", [OWN, D], F32, kind="ExternalOutput")
    for n, sh in (("kscrh", [NH, 128, S]), ("kscrl", [NH, 128, S]),
                  ("vscrh", [NH, 128, S]), ("vscrl", [NH, 128, S]),
                  ("qscrh", [NH, 128, OWN]), ("qscrl", [NH, 128, OWN]),
                  ("ascrh", [NQT, 2, 128, NH * 128]), ("ascrl", [NQT, 2, 128, NH * 128]),
                  ("woscrh", [FC, 128, D]), ("woscrl", [FC, 128, D]),
                  ("hqscrh", [NJ, 128, FF]), ("hqscrl", [NJ, 128, FF]),
                  ):
        T[n] = nc.dram_tensor(n, sh, BF16)
    T["x1scr"] = nc.dram_tensor("x1scr", [NJ, 128, D], F32)
    for n, sh in (("xnscrh", [32, 128, D]), ("xnscrl", [32, 128, D]),
                  ("xqscrh", [NJ, 128, D]), ("xqscrl", [NJ, 128, D]),
                  ("gwsh", [FF, D]), ("gwsl", [FF, D]),
                  ("uwsh", [FF, D]), ("uwsl", [FF, D]),
                  ("dwsh", [D, FF]), ("dwsl", [D, FF]),
                  ("dwfmh", [FC, 128, FF]), ("dwfml", [FC, 128, FF]),
                  ("hqfmh", [NJ, 128, FF]), ("hqfml", [NJ, 128, FF]),
                  ("dpscr", [2, 3, 128, 512])):
        T[n] = nc.dram_tensor(n, sh, BF16)
    for name, (shape, dt) in DEBUG_TAPS.items():
        T[name] = nc.dram_tensor(name, list(shape), dt,
                                 kind="ExternalOutput")

    p = Prog()
    es = contextlib.ExitStack()
    with es:
        sb = {}
        sb["a32"] = es.enter_context(nc.sbuf_tensor("a32", [128, 14336], F32))
        sb["a16"] = es.enter_context(nc.sbuf_tensor("a16", [128, 24576], BF16))
        sb["xqf"] = es.enter_context(nc.sbuf_tensor("xqf", [128, 32768], BF16))
        sb["cmsk"] = es.enter_context(nc.sbuf_tensor("cmsk", [128, 4 * QT], F32))
        sb["cbor"] = es.enter_context(nc.sbuf_tensor("cbor", [128, D], F32))
        sb["cgr"] = es.enter_context(nc.sbuf_tensor("cgr", [128, D], F32))
        sb["cbr"] = es.enter_context(nc.sbuf_tensor("cbr", [128, D], F32))
        sb["cbq"] = es.enter_context(nc.sbuf_tensor("cbq", [128, FC], F32))
        sb["cbk"] = es.enter_context(nc.sbuf_tensor("cbk", [128, FC], F32))
        sb["cbv"] = es.enter_context(nc.sbuf_tensor("cbv", [128, FC], F32))
        sb["coc"] = es.enter_context(nc.sbuf_tensor("coc", [128, 1], BF16))
        sb["cor"] = es.enter_context(nc.sbuf_tensor("cor", [1, 128], BF16))
        sb["stat"] = es.enter_context(nc.sbuf_tensor("stat", [128, 288], F32))
        sb["srow"] = es.enter_context(nc.sbuf_tensor("srow", [1, QT], F32))
        sb["srwb"] = es.enter_context(nc.sbuf_tensor("srwb", [1, 2 * QT], BF16))
        sb["q8t"] = es.enter_context(nc.sbuf_tensor("q8t", [128, 2048], FP8))
        sb["ps"] = [es.enter_context(nc.psum_tensor(f"ps{i}", [128, 512], F32))
                    for i in range(8)]
        sem_names = ("LD", "TT", "ST", "PE", "VE", "AC", "LW", "LR",
                     "L3", "L4", "LX")
        sems = {s: es.enter_context(nc.semaphore(s)) for s in sem_names}

        emit_all(p, nc, T, sb)
        p.check()

        blk = es.enter_context(nc.Block())
        handles = dict(pe=blk.tensor, ve=blk.vector, ac=blk.scalar,
                       sy=blk.sync, gp=blk.gpsimd)
        for e in ENGS:
            stream = p.streams[e]

            def replay(eng, stream=stream):
                last = {}
                for w, fn, inc, n, tag in stream:
                    for s, v in w:
                        if last.get(s, -1) < v:
                            eng.wait_ge(sems[s], v)
                            last[s] = v
                    ins = fn(eng)
                    if inc is not None:
                        ins.then_inc(sems[inc], n)
            handles[e](replay)
    return nc


def emit_all(p, nc, T, sb):
    a32, a16, xqf = sb["a32"], sb["a16"], sb["xqf"]
    ps = sb["ps"]
    stat, srow, srwb, q8t = sb["stat"], sb["srow"], sb["srwb"], sb["q8t"]
    ap = {n: h.ap() for n, h in T.items()}
    st = p.state

    def dma(eng, out, in_, sem, waits=(), tag=""):
        p.emit(eng, lambda e, o=out, i=in_: e.dma_start(out=o, in_=i),
               waits=waits, inc=sem, n=16, tag=tag)
        return p.tok(sem)

    def dmat(out, in_, waits=(), tag=""):
        p.emit("ac", lambda e, o=out, i=in_:
               e.dma_start(out=o, in_=i, transpose=True),
               waits=waits, inc="TT", n=16, tag=tag)
        return p.tok("TT")

    def ve(fn, waits=(), tag="", dr=True):
        if dr:
            p.emit("ve", fn, waits=waits, tag=tag)
            p.emit("ve", lambda e: e.drain(), inc="VE", tag=tag + ".d")
        else:
            p.emit("ve", fn, waits=waits, inc="VE", tag=tag)
        return p.tok("VE")

    def ac(fn, waits=(), tag="", dr=True):
        if dr:
            p.emit("ac", fn, waits=waits, tag=tag)
            p.emit("ac", lambda e: e.drain(), inc="AC", tag=tag + ".d")
        else:
            p.emit("ac", fn, waits=waits, inc="AC", tag=tag)
        return p.tok("AC")

    def pe(fn, waits=(), inc=None, tag=""):
        p.emit("pe", fn, waits=waits, inc=inc, tag=tag)
        return p.tok("PE")

    def tap(name, src, waits, tag="tap"):
        if name in DEBUG_TAPS:
            dma("gp", ap[name], src, "ST", waits, tag)

    dma("sy", sb["cmsk"][:, :], ap["mask4"], "LD")
    dma("sy", sb["cbor"][:, :], ap["bo2"].broadcast_to([128, D]), "LD")
    dma("sy", sb["cgr"][:, :], ap["lng"].broadcast_to([128, D]), "LD")
    dma("sy", sb["cbr"][:, :], ap["lnb"].broadcast_to([128, D]), "LD")
    dma("sy", sb["cbq"][:, :], ap["bq16"], "LD")
    dma("sy", sb["cbk"][:, :], ap["bk16"], "LD")
    dma("sy", sb["cbv"][:, :], ap["bv16"], "LD")
    dma("sy", sb["coc"][:, :], ap["onescol"], "LD")
    CONST = dma("sy", sb["cor"][:, :], ap["onesrow"], "LD")
    tap("t_cgr", sb["cgr"][:, :], [("LD", CONST)])

    def ln_chain(src_ap, dump_ap, out_ap, pre_waits, tag):
        mus, ngm, ve_, rec, rsq, ssq = (stat[:, i:i + 1] for i in range(6))
        w0 = list(pre_waits) + [("AC", st.get("ln_ac", 0))]
        ve(lambda e, i=src_ap, o=mus: e.reduce_sum(o, i, AX),
           waits=w0, tag=tag + ".sum")
        vt = ve(lambda e, o=ngm, i=mus: e.tensor_scalar_mul(o, i, -INV_D))
        at = ac(lambda e, i=src_ap, o=dump_ap, b=ngm, s=ssq:
                e.activation(o, i, AF.Square, bias=b, scale=1.0,
                             accum_out=s),
                waits=[("VE", vt), ("LD", CONST)], tag=tag + ".sq")
        ve(lambda e, o=ve_, i=ssq: e.tensor_scalar(o, i, INV_D, EPS,
                                                   ALU.mult, ALU.add),
           waits=[("AC", at)])
        vt = ve(lambda e, o=rec, i=ve_: e.reciprocal(o, i))
        at = ac(lambda e, o=rsq, i=rec: e.activation(o, i, AF.Sqrt),
                waits=[("VE", vt)])
        st["ln_ac"] = at
        r_ = ve(lambda e, o=out_ap, i=src_ap, s1=ngm, s2=rsq:
                e.tensor_scalar(o, i, s1, s2, ALU.add, ALU.mult),
                waits=[("AC", at)], tag=tag + ".nrm")
        tap("t_stat_" + tag, stat[:, :8], [("VE", r_)])
        return r_

    def mxq_chain(src_ap, out_ap, n, pre_waits, tag):
        nb = n // 32
        am = stat[:, 8:8 + nb]
        s_ = stat[:, 8 + nb:8 + 2 * nb]
        rs2 = stat[:, 8 + 2 * nb:8 + 3 * nb]
        s2 = stat[:, 8 + 3 * nb:8 + 4 * nb]
        q8 = q8t[:, :n]
        ve(lambda e, i=src_ap, o=am: e.tensor_reduce(
            o.rearrange("q (b o) -> q b o", o=1),
            i.rearrange("q (b i) -> q b i", i=32),
            AX, ALU.max, apply_absolute_value=True),
            waits=pre_waits, tag=tag + ".amax")
        ve(lambda e, o=s_, i=am: e.tensor_scalar(o, i, C448, EPS30,
                                                 ALU.mult, ALU.max))
        ve(lambda e, o=rs2, i=s_: e.reciprocal(o, i))
        ve(lambda e, o=rs2, i=rs2: e.tensor_scalar_mul(o, i, 0.5))
        ve(lambda e, o=s2, i=s_: e.tensor_scalar_mul(o, i, 2.0))
        ve(lambda e, o=q8, i=src_ap, s=rs2, b=nb: e.tensor_tensor(
            o.rearrange("q (k i) -> q k i", i=32),
            i.rearrange("q (k i) -> q k i", i=32),
            s.unsqueeze(2).broadcast_to([128, b, 32]), ALU.mult),
            tag=tag + ".q8")
        return ve(lambda e, o=out_ap, i=q8, s=s2, b=nb: e.tensor_tensor(
            o.rearrange("q (k i) -> q k i", i=32),
            i.rearrange("q (k i) -> q k i", i=32),
            s.unsqueeze(2).broadcast_to([128, b, 32]), ALU.mult),
            tag=tag + ".deq")

    XRAW, LNS, XN, WF, EV = 0, 4096, 6144, 8192, 12288
    XNH, XNL, WH, WL, EVH, EVL = 0, 2048, 4096, 8192, 12288, 14336
    pan_pe = 0
    for pan in range(4):
        tt_tok = {}
        for tl in range(8):
            row0 = pan * 1024 + tl * 128
            slot = tl % 2
            xr = a32[:, XRAW + slot * D: XRAW + slot * D + D]
            ldt = dma("sy", xr, ap["xb"][row0:row0 + 128, :], "LD",
                      [("VE", st.get(("xr", slot), 0))], f"x{pan}.{tl}")
            tap(f"t_xr_{pan}_{tl}", xr, [("LD", ldt)])
            nt = ln_chain(xr, a32[:, LNS:LNS + D], a32[:, XN:XN + D],
                          [("LD", ldt)], f"ln1.{pan}.{tl}")
            xn = a32[:, XN:XN + D]
            ve(lambda e, o=xn, i=xn, c=sb["cgr"][:, :]:
               e.tensor_tensor(o, i, c, ALU.mult))
            ve(lambda e, o=xn, i=xn, c=sb["cbr"][:, :]:
               e.tensor_tensor(o, i, c, ALU.add))
            xh = a16[:, XNH:XNH + D]
            xl = a16[:, XNL:XNL + D]
            ve(lambda e, o=xh, i=xn: e.tensor_copy(o, i),
               waits=[("ST", st.get("xnsp", 0))])
            vt = ve(lambda e, o=xl, a=xn, b=xh:
                    e.tensor_tensor(o, a, b, ALU.subtract))
            st[("xr", slot)] = vt
            j32 = pan * 8 + tl
            dma("gp", ap["xnscrh"][j32], xh, "ST", [("VE", vt)], f"spxn{j32}")
            xnsp = dma("gp", ap["xnscrl"][j32], xl, "ST")
            st["xnsp"] = xnsp
            if pan == 0 and tl == 0 and "t_pin_xn" in DEBUG_TAPS:
                dma("gp", ap["t_pin_xn"], xn, "ST", [("VE", vt)], "pinxn")
                dma("gp", ap["t_pin_xh"], xh, "ST", [("VE", vt)], "pinxh")
                dma("gp", ap["t_pin_st"], stat[:, :8], "ST", [("VE", vt)])
                pintok = p.tok("ST")
                p.pending["ve"].append(("ST", pintok))
                p.pending["ac"].append(("ST", pintok))
            tap(f"t_xn_{pan}_{tl}", xn, [("VE", vt)])
            for s, scrn in ((0, "xnscrh"), (1, "xnscrl")):
                dst = _ap(xqf, s * 16384 + tl * 128, [[1024, FC], [1, 128]])
                dmat(dst, ap[scrn][j32].rearrange("q (c f) -> q c f", f=128),
                     [("ST", xnsp), ("PE", pan_pe)], f"xnfT{pan}.{tl}.{s}")
            tt_tok[tl] = p.tok("TT")
        xnf_rdy = p.tok("TT")
        if pan == 0:
            tap("t_xnfh", xqf[:, :16384], [("TT", xnf_rdy)])
            tap("t_xnfl", xqf[:, 16384:], [("TT", xnf_rdy)])

        for wn, bias_t, dsth, dstl, scale, is_q in (
                ("wk", "cbk", "kscrh", "kscrl", None, False),
                ("wv", "cbv", "vscrh", "vscrl", None, False),
                ("wq", "cbq", "qscrh", "qscrl", RSQ, True)):
            for oc in range(FC):
                db = oc % 2
                wf = a32[:, WF + db * D: WF + db * D + D]
                wh = a16[:, WH + db * D: WH + db * D + D]
                wl = a16[:, WL + db * D: WL + db * D + D]
                wsrc = bass.AP(T[wn], oc * 128,
                               [[D, 128], [128 * D, FC], [1, 128]])
                lw = dma("sy", wf.rearrange("q (c f) -> q c f", f=128), wsrc,
                         "LW", [("VE", st.get(("wf", db), 0))], f"{wn}{oc}")
                ve(lambda e, o=wh, i=wf: e.tensor_copy(o, i),
                   waits=[("LW", lw), ("PE", st.get(("whl", db), 0))])
                wtok = ve(lambda e, o=wl, a=wf, b=wh:
                          e.tensor_tensor(o, a, b, ALU.subtract))
                st[("wf", db)] = wtok
                for hh in range(1 if is_q else 2):
                    bk = (oc * 2 + hh) % 4
                    bank = ps[bk][:, :]
                    for fc in range(FC):
                        if is_q:
                            rh = _ap(xqf, fc * 1024, [[256, 4], [1, 128]])
                            rl = _ap(xqf, 16384 + fc * 1024,
                                     [[256, 4], [1, 128]])
                        else:
                            off = fc * 1024 + hh * 512
                            rh = xqf[:, off:off + 512]
                            rl = xqf[:, 16384 + off:16384 + off + 512]
                        lh = wh[:, fc * 128:(fc + 1) * 128]
                        ll = wl[:, fc * 128:(fc + 1) * 128]
                        w0 = ([("VE", wtok), ("TT", xnf_rdy),
                               ("VE", st.get(("bank", bk), 0))]
                              if fc == 0 else ())
                        pe(lambda e, o=bank, a=lh, b=rh, s=(fc == 0):
                           e.matmul(o, a, b, start=s, stop=False), waits=w0)
                        pe(lambda e, o=bank, a=lh, b=rl:
                           e.matmul(o, a, b, start=False, stop=False))
                        petok = pe(lambda e, o=bank, a=ll, b=rh,
                                   q=(fc == FC - 1):
                                   e.matmul(o, a, b, start=False, stop=q),
                                   inc="PE" if fc == FC - 1 else None)
                    st[("whl", db)] = petok
                    eb = a32[:, EV + db * 512: EV + db * 512 + 512]
                    eh = a16[:, EVH + db * 512: EVH + db * 512 + 512]
                    el = a16[:, EVL + db * 512: EVL + db * 512 + 512]
                    bias = sb[bias_t][:, oc:oc + 1]
                    ww = [("PE", petok), ("ST", st.get(("ev", db), 0))]
                    if scale is None:
                        st[("bank", bk)] = ve(
                            lambda e, o=eb, i=bank, b=bias:
                            e.tensor_scalar(o, i, b, None, ALU.add),
                            waits=ww, tag=f"ev{wn}{oc}", dr=False)
                    else:
                        st[("bank", bk)] = ve(
                            lambda e, o=eb, i=bank, b=bias:
                            e.tensor_scalar(o, i, b, scale, ALU.add,
                                            ALU.mult),
                            waits=ww, tag=f"ev{wn}{oc}", dr=False)
                    ve(lambda e, o=eh, i=eb: e.tensor_copy(o, i))
                    vt = ve(lambda e, o=el, a=eb, b=eh:
                            e.tensor_tensor(o, a, b, ALU.subtract))
                    col0 = pan * 512 if is_q else pan * 1024 + hh * 512
                    for src, dn in ((eh, dsth), (el, dstl)):
                        dma("gp", ap[dn][oc][:, col0:col0 + 512], src, "ST",
                            [("VE", vt)], f"sp{wn}{oc}")
                    st[("ev", db)] = p.tok("ST")
        pan_pe = p.tok("PE")

    P1ST = p.tok("ST")
    RGH, RGL, SHL, AVHL, DWB = 8192, 9216, 10240, 10752, 11264
    RGF, SAC, BSB, AVN, DWF, DWD, WOF = 0, 1024, 1280, 1536, 2048, 6144, 8192

    def load_head(h):
        slot = h % 2
        base = slot * 16384
        w = [("PE", st.get(("kvq", slot), 0)), ("ST", P1ST)]
        dma("sy", xqf[:, base:base + 4096], ap["kscrh"][h], "LD", w, f"kh{h}")
        dma("sy", xqf[:, base + 4096:base + 8192], ap["kscrl"][h], "LD")
        dmat(xqf[:, base + 8192:base + 12288].rearrange(
            "q (c f) -> q c f", f=128),
             ap["vscrh"][h].rearrange("q (c f) -> q c f", f=128), w, f"vT{h}")
        dmat(xqf[:, base + 12288:base + 16384].rearrange(
            "q (c f) -> q c f", f=128),
             ap["vscrl"][h].rearrange("q (c f) -> q c f", f=128))
        dma("sy", a16[:, slot * 4096:slot * 4096 + 2048], ap["qscrh"][h],
            "LD", w, f"qh{h}")
        dma("sy", a16[:, slot * 4096 + 2048:slot * 4096 + 4096],
            ap["qscrl"][h], "LD")
        st[("kvqld", slot)] = (p.tok("LD"), p.tok("TT"))

    load_head(0)
    for h in range(NH):
        if h + 1 < NH:
            load_head(h + 1)
        slot = h % 2
        base = slot * 16384
        ldt, ttt = st[("kvqld", slot)]
        qh = a16[:, slot * 4096: slot * 4096 + 2048]
        ql = a16[:, slot * 4096 + 2048: slot * 4096 + 4096]
        for t in range(NQT):
            nkc = 4 * (t + 1)
            qhi = qh[:, t * QT:(t + 1) * QT]
            qlo = ql[:, t * QT:(t + 1) * QT]
            avb = 2 + (t % 2)
            av = ps[avb][:, :QT]
            def emit_av(kc, vt):
                r = kc % 4
                vh_ = xqf[:, base + 8192 + kc * 128:
                          base + 8192 + (kc + 1) * 128]
                vl_ = xqf[:, base + 12288 + kc * 128:
                          base + 12288 + (kc + 1) * 128]
                rh = a16[:, RGH + r * QT: RGH + (r + 1) * QT]
                rl = a16[:, RGL + r * QT: RGL + (r + 1) * QT]
                pe(lambda e, o=av, a=vh_, b=rh, s=(kc == 0):
                   e.matmul(o, a, b, start=s, stop=False),
                   waits=[("VE", vt)])
                pe(lambda e, o=av, a=vh_, b=rl:
                   e.matmul(o, a, b, start=False, stop=False))
                st[("ring", kc % 4)] = pe(
                    lambda e, o=av, a=vl_, b=rh, q=(kc == nkc - 1):
                    e.matmul(o, a, b, start=False, stop=q), inc="PE")

            pend = []
            for kc in range(nkc):
                sbk = kc % 2
                sbank = ps[sbk][:, :QT]
                r = kc % 4
                kh_ = xqf[:, base + kc * 128: base + (kc + 1) * 128]
                kl_ = xqf[:, base + 4096 + kc * 128: base + 4096 + (kc + 1) * 128]
                w0 = [("LD", ldt), ("TT", ttt),
                      ("AC", st.get(("sbank", sbk), 0))]
                pe(lambda e, o=sbank, a=kh_, b=qhi:
                   e.matmul(o, a, b, start=True, stop=False), waits=w0,
                   tag=f"sc{h}.{t}.{kc}")
                pe(lambda e, o=sbank, a=kh_, b=qlo:
                   e.matmul(o, a, b, start=False, stop=False))
                sct = pe(lambda e, o=sbank, a=kl_, b=qhi:
                         e.matmul(o, a, b, start=False, stop=True), inc="PE")
                aw = [("PE", sct)]
                if kc >= nkc - 4:
                    mi = kc - (nkc - 4)
                    vt = ve(lambda e, o=sbank,
                            m=sb["cmsk"][:, mi * QT:(mi + 1) * QT]:
                            e.tensor_tensor(o, o, m, ALU.add),
                            waits=[("PE", sct)], tag=f"msk{kc}")
                    aw = [("VE", vt)]
                rh = a16[:, RGH + r * QT: RGH + (r + 1) * QT]
                rl = a16[:, RGL + r * QT: RGL + (r + 1) * QT]
                rf = a32[:, RGF + r * QT: RGF + (r + 1) * QT]
                aw.append(("PE", st.get(("ring", r), 0)))
                ac(lambda e, o=rh, i=sbank: e.activation(o, i, AF.Exp),
                   waits=aw, tag=f"exp{h}.{t}.{kc}", dr=False)
                st[("sbank", sbk)] = ac(lambda e, o=rf, i=sbank:
                                        e.activation(o, i, AF.Exp), dr=False)
                atok = p.tok("AC")
                ve(lambda e, o=rl, a=rf, b=rh:
                   e.tensor_tensor(o, a, b, ALU.subtract),
                   waits=[("AC", atok)], dr=False)
                if kc == 0:
                    vt = ve(lambda e, o=a32[:, SAC:SAC + QT], i=rf:
                            e.tensor_copy(o, i), dr=False)
                else:
                    vt = ve(lambda e, o=a32[:, SAC:SAC + QT], i=rf:
                            e.tensor_tensor(o, o, i, ALU.add), dr=False)
                pend.append((kc, vt))
                if len(pend) > 2:
                    emit_av(*pend.pop(0))
            for item in pend:
                emit_av(*item)
            avpe = p.tok("PE")
            sh_ = a16[:, SHL:SHL + QT]
            sl_ = a16[:, SHL + QT:SHL + 2 * QT]
            ve(lambda e, o=sh_, i=a32[:, SAC:SAC + QT]: e.tensor_copy(o, i),
               waits=[("PE", st.get("shl_pe", 0))], dr=False)
            vt = ve(lambda e, o=sl_, a=a32[:, SAC:SAC + QT], b=sh_:
                    e.tensor_tensor(o, a, b, ALU.subtract))
            rowp = ps[4][:1, :QT]
            pe(lambda e, o=rowp, a=sb["coc"][:, :], b=sh_:
               e.matmul(o, a, b, start=True, stop=False),
               waits=[("VE", vt), ("VE", st.get("row_free", 0)),
                      ("LD", CONST)])
            rt = pe(lambda e, o=rowp, a=sb["coc"][:, :], b=sl_:
                    e.matmul(o, a, b, start=False, stop=True), inc="PE")
            st["shl_pe"] = rt
            vt = ve(lambda e, o=srow[:1, :QT], i=rowp: e.reciprocal(o, i),
                    waits=[("PE", rt)], tag=f"rcp{h}.{t}")
            st["row_free"] = vt
            ve(lambda e, o=srwb[:1, :QT], i=srow[:1, :QT]:
               e.tensor_copy(o, i), waits=[("PE", st.get("srwb_pe", 0))],
               dr=False)
            vt = ve(lambda e, o=srwb[:1, QT:2 * QT], a=srow[:1, :QT],
                    b=srwb[:1, :QT]: e.tensor_tensor(o, a, b, ALU.subtract))
            bp = ps[5][:, :QT]
            pe(lambda e, o=bp, a=sb["cor"][:1, :], b=srwb[:1, :QT]:
               e.matmul(o, a, b, start=True, stop=False),
               waits=[("VE", vt), ("VE", st.get("b_free", 0))])
            bt_ = pe(lambda e, o=bp, a=sb["cor"][:1, :],
                     b=srwb[:1, QT:2 * QT]:
                     e.matmul(o, a, b, start=False, stop=True), inc="PE")
            st["srwb_pe"] = bt_
            ve(lambda e, o=a32[:, BSB:BSB + QT], i=bp: e.tensor_copy(o, i),
               waits=[("PE", bt_)], dr=False)
            st["b_free"] = p.tok("VE")
            avn = a32[:, AVN:AVN + QT]
            vt = ve(lambda e, o=avn, a=av, b=a32[:, BSB:BSB + QT]:
                    e.tensor_tensor(o, a, b, ALU.mult),
                    waits=[("PE", avpe)], tag=f"avn{h}.{t}", dr=False)
            ah_ = a16[:, AVHL:AVHL + QT]
            al_ = a16[:, AVHL + QT:AVHL + 2 * QT]
            ve(lambda e, o=ah_, i=avn: e.tensor_copy(o, i),
               waits=[("ST", st.get("spav", 0))])
            vt = ve(lambda e, o=al_, a=avn, b=ah_:
                    e.tensor_tensor(o, a, b, ALU.subtract))
            tap(f"t_avn_{h}_{t}", avn, [("VE", vt)])
            dsth_ = bass.AP(T["ascrh"], t * (2 * 128 * NH * 128) + h * 128,
                            [[NH * 128, 128], [128 * NH * 128, 2], [1, 128]])
            dstl_ = bass.AP(T["ascrl"], t * (2 * 128 * NH * 128) + h * 128,
                            [[NH * 128, 128], [128 * NH * 128, 2], [1, 128]])
            dma("gp", dsth_, ah_.rearrange("q (c f) -> q c f", f=128), "ST",
                [("VE", vt)], f"spav{h}.{t}")
            st["spav"] = dma("gp", dstl_,
                             al_.rearrange("q (c f) -> q c f", f=128), "ST")
            if h == 0 and t == 0 and "t_p2vh" in DEBUG_TAPS:
                vt0 = p.tok("VE")
                dma("gp", ap["t_p2vh"], xqf[:, base + 8192:base + 12288],
                    "ST", [("VE", vt0)], "dbgvh")
                dma("gp", ap["t_p2rgf"], a32[:, RGF:RGF + 1024], "ST",
                    [("VE", vt0)])
                dma("gp", ap["t_p2sac"], a32[:, SAC:SAC + QT], "ST",
                    [("VE", vt0)])
                dma("gp", ap["t_p2bsb"], a32[:, BSB:BSB + QT], "ST",
                    [("VE", vt0)])
                dma("gp", ap["t_p2avn"], a32[:, AVN:AVN + QT], "ST",
                    [("VE", vt0)])
                dma("gp", ap["t_p2qh"], a16[:, slot * 4096:slot * 4096 + 2048],
                    "ST", [("VE", vt0)])
                st["p2dbg"] = p.tok("ST")
                p.pending["ve"].append(("ST", st["p2dbg"]))
                p.pending["ac"].append(("ST", st["p2dbg"]))
                p.pending["pe"].append(("ST", st["p2dbg"]))
        st[("kvq", slot)] = p.tok("PE")

        wof = a32[:, WOF:WOF + 2048]
        woh = a16[:, DWB:DWB + 2048]
        wol = a16[:, DWB + 2048:DWB + 4096]
        lw = dma("sy", wof, ap["wo"][h * 128:(h + 1) * 128, :], "LR",
                 [("VE", st.get("wof_ve", 0))], f"wo{h}")
        ve(lambda e, o=woh, i=wof: e.tensor_copy(o, i),
           waits=[("LW", 0), ("LR", lw), ("ST", st.get("wo_st", 0)),
                  ("TT", st.get("dwt", 0))])
        vt = ve(lambda e, o=wol, a=wof, b=woh:
                e.tensor_tensor(o, a, b, ALU.subtract))
        st["wof_ve"] = vt
        dma("gp", ap["woscrh"][h], woh, "ST", [("VE", vt)], f"spwo{h}")
        st["wo_st"] = dma("gp", ap["woscrl"][h], wol, "ST")
        for qtr in range(4):
            dwf = a32[:, DWF:DWF + 2048]
            dwd = a32[:, DWD:DWD + 2048]
            dh = a16[:, DWB + 4096:DWB + 6144]
            dl = a16[:, DWB + 6144:DWB + 8192]
            srcq = ap["dw"][h * 128:(h + 1) * 128, qtr * 2048:(qtr + 1) * 2048]
            lw = dma("sy", dwf, srcq, "LR", [("VE", st.get("dwf_ve", 0))],
                     f"dw{h}.{qtr}")
            st["dwf_ve"] = mxq_chain(dwf, dwd, 2048, [("LR", lw)],
                                     f"dwq{h}.{qtr}")
            ve(lambda e, o=dh, i=dwd: e.tensor_copy(o, i),
               waits=[("ST", st.get("dwsp", 0))])
            vt = ve(lambda e, o=dl, a=dwd, b=dh:
                    e.tensor_tensor(o, a, b, ALU.subtract))
            dma("gp", ap["dwsh"][h * 128:(h + 1) * 128,
                                 qtr * 2048:(qtr + 1) * 2048],
                dh, "ST", [("VE", vt)], f"spdw{h}")
            st["dwsp"] = dma("gp",
                             ap["dwsl"][h * 128:(h + 1) * 128,
                                        qtr * 2048:(qtr + 1) * 2048],
                             dl, "ST")
        for s_i, sn, fmn in ((0, "dwsh", "dwfmh"), (1, "dwsl", "dwfml")):
            strip = a16[:, DWB:DWB + 8192]
            tws = dmat(strip.rearrange("q (c f) -> q c f", f=128),
                       ap[sn][h * 128:(h + 1) * 128, :].rearrange(
                           "q (c f) -> q c f", f=128),
                       [("ST", st["dwsp"]), ("ST", st.get("dwfmsp", 0))],
                       f"dwfmT{h}.{s_i}")
            st["dwfmsp"] = dma("gp", ap[fmn][h], strip,
                               "ST", [("TT", tws)], f"dwfmsp{h}.{s_i}")

    st["p2_pe"] = p.tok("PE")
    emit_p34(p, nc, T, sb, ap, dma, dmat, ve, ac, pe, tap, ln_chain,
             mxq_chain, CONST, _ap)


def emit_p34(p, nc, T, sb, ap, dma, dmat, ve, ac, pe, tap, ln_chain,
             mxq_chain, CONST, _ap_):
    a32, a16, xqf = sb["a32"], sb["a16"], sb["xqf"]
    ps = sb["ps"]
    st = p.state
    p.barrier()

    OT, X1, LNS3, XQ = 0, 2048, 4096, 6144
    AVT, WOS, XQH = 0, 4096, 8192
    GWTH, GWTL, UWTH, UWTL, PRET, HQB, HQT = (0, 4096, 8192, 12288,
                                              16384, 20480, 21504)
    WFQ, WDQ, SIL, HB = 0, 2048, 4096, 4608

    st["xqf_pe"] = st.get("p2_pe", 0)
    for half in range(2):
        p.barrier()
        for tt in [0] + list(range(8)):
            j = half * 8 + tt
            w = [("PE", st.get("avt_pe", 0))]
            dma("sy", a16[:, AVT:AVT + 2048], ap["ascrh"][j // 2][j % 2],
                "L3", w, f"av{j}")
            lda = dma("sy", a16[:, AVT + 2048:AVT + 4096],
                      ap["ascrl"][j // 2][j % 2], "L3")
            otile = a32[:, OT:OT + D]
            for oc in range(4):
                bk = oc % 4
                bank = ps[bk][:, :]
                for fc in range(FC):
                    db = fc % 2
                    rh = a16[:, WOS + db * 1024: WOS + db * 1024 + 512]
                    rl = a16[:, WOS + db * 1024 + 512:
                             WOS + db * 1024 + 1024]
                    dma("sy", rh,
                        ap["woscrh"][fc][:, oc * 512:(oc + 1) * 512],
                        "LR", [("PE", st.get(("wos", db), 0))],
                        f"wo{j}.{oc}.{fc}")
                    lw = dma("sy", rl,
                             ap["woscrl"][fc][:, oc * 512:(oc + 1) * 512],
                             "LR")
                    lh = a16[:, AVT + fc * 128: AVT + (fc + 1) * 128]
                    ll = a16[:, AVT + 2048 + fc * 128:
                             AVT + 2048 + (fc + 1) * 128]
                    w0 = [("LR", lw), ("L3", lda)]
                    if fc == 0:
                        w0.append(("VE", st.get(("p3bank", bk), 0)))
                    pe(lambda e, o=bank, a=lh, b=rh, s=(fc == 0):
                       e.matmul(o, a, b, start=s, stop=False), waits=w0)
                    pe(lambda e, o=bank, a=lh, b=rl:
                       e.matmul(o, a, b, start=False, stop=False))
                    petok = pe(lambda e, o=bank, a=ll, b=rh,
                               q=(fc == FC - 1):
                               e.matmul(o, a, b, start=False, stop=q),
                               inc="PE")
                    st[("wos", db)] = petok
                st[("p3bank", bk)] = ve(
                    lambda e, o=otile[:, oc * 512:(oc + 1) * 512], i=bank:
                    e.tensor_copy(o, i), waits=[("PE", petok)],
                    tag=f"oev{j}.{oc}", dr=False)
            st["avt_pe"] = petok
            vt = ve(lambda e, o=otile, i=otile, c=sb["cbor"][:, :]:
                    e.tensor_tensor(o, i, c, ALU.add), dr=False)
            ldx = dma("sy", a32[:, LNS3:LNS3 + D],
                      ap["xb"][j * 256: j * 256 + 128, :], "LX",
                      [("AC", st.get("ln_ac", 0))], f"xo{j}")
            vt = ve(lambda e, o=otile, a=otile, b=a32[:, LNS3:LNS3 + D]:
                    e.tensor_tensor(o, a, b, ALU.add), waits=[("LX", ldx)])
            x1 = a32[:, X1:X1 + D]
            vt = ln_chain(otile, a32[:, LNS3:LNS3 + D], x1,
                          [("VE", vt), ("ST", st.get("x1st", 0))],
                          f"ln2.{j}")
            tap(f"t_x1_{j}", x1, [("VE", vt)])
            st["x1st"] = dma("gp", ap["x1scr"][j], x1, "ST", [("VE", vt)],
                             f"spx1{j}")
            xq = a32[:, XQ:XQ + D]
            vt = mxq_chain(x1, xq, D, [("VE", vt)], f"xq{j}")
            xh = a16[:, XQH:XQH + D]
            xl = a16[:, XQH + D:XQH + 2 * D]
            ve(lambda e, o=xh, i=xq: e.tensor_copy(o, i),
               waits=[("ST", st.get("xqsp", 0))])
            vt = ve(lambda e, o=xl, a=xq, b=xh:
                    e.tensor_tensor(o, a, b, ALU.subtract))
            dma("gp", ap["xqscrh"][j], xh, "ST", [("VE", vt)], f"spxq{j}")
            xqsp = dma("gp", ap["xqscrl"][j], xl, "ST")
            st["xqsp"] = xqsp
            for s, scrn in ((0, "xqscrh"), (1, "xqscrl")):
                dst = _ap_(xqf, s * 16384 + tt * 128, [[1024, FC], [1, 128]])
                dmat(dst, ap[scrn][j].rearrange("q (c f) -> q c f", f=128),
                     [("ST", xqsp), ("PE", st["xqf_pe"])], f"xqT{j}.{s}")
            st["xqT"] = p.tok("TT")
        xqf_rdy = p.tok("TT")

        p.barrier()
        for wg in range(32):
            tts = 0
            for wname, bh, bl, sh_n, sl_n in (
                    ("gw", GWTH, GWTL, "gwsh", "gwsl"),
                    ("uw", UWTH, UWTL, "uwsh", "uwsl")):
                for m in range(2):
                    r0 = wg * 256 + m * 128
                    if half == 0:
                        wfq = a32[:, WFQ:WFQ + D]
                        wdq = a32[:, WDQ:WDQ + D]
                        ph_ = a16[:, PRET:PRET + D]
                        pl_ = a16[:, PRET + D:PRET + 2 * D]
                        lw = dma("sy", wfq, ap[wname][r0:r0 + 128, :], "L4",
                                 [("VE", st.get("wfq_ve", 0))],
                                 f"{wname}{wg}.{m}")
                        st["wfq_ve"] = mxq_chain(wfq, wdq, D, [("L4", lw)],
                                                 f"{wname}q{wg}.{m}")
                        ve(lambda e, o=ph_, i=wdq: e.tensor_copy(o, i),
                           waits=[("ST", st.get("wssp", 0))])
                        vt = ve(lambda e, o=pl_, a=wdq, b=ph_:
                                e.tensor_tensor(o, a, b, ALU.subtract))
                        dma("gp", ap[sh_n][r0:r0 + 128, :], ph_, "ST",
                            [("VE", vt)], f"spw{wname}{wg}.{m}")
                        wssp = dma("gp", ap[sl_n][r0:r0 + 128, :], pl_, "ST")
                        st["wssp"] = wssp
                    else:
                        wssp = 0
                    for s_i, scrn, bbase in ((0, sh_n, bh), (1, sl_n, bl)):
                        srcap = bass.AP(T[scrn], r0 * D,
                                        [[D, 128], [128, FC], [1, 128]])
                        tts = dmat(
                            _ap_(a16, bbase + m * 128, [[256, FC], [1, 128]]),
                            srcap,
                            [("ST", wssp), ("PE", st.get("gu_pe", 0))],
                            f"{wname}T{wg}.{m}")
                    st["pretT"] = tts
            wrdy = tts
            for tt in range(8):
                gb, ub = ps[(tt % 2) * 2], ps[(tt % 2) * 2 + 1]
                gps, ups = gb[:, :256], ub[:, :256]
                for fc in range(FC):
                    lh = xqf[:, fc * 1024 + tt * 128:
                             fc * 1024 + (tt + 1) * 128]
                    ll = xqf[:, 16384 + fc * 1024 + tt * 128:
                             16384 + fc * 1024 + (tt + 1) * 128]
                    grh = a16[:, GWTH + fc * 256: GWTH + (fc + 1) * 256]
                    grl = a16[:, GWTL + fc * 256: GWTL + (fc + 1) * 256]
                    urh = a16[:, UWTH + fc * 256: UWTH + (fc + 1) * 256]
                    url = a16[:, UWTL + fc * 256: UWTL + (fc + 1) * 256]
                    w0 = []
                    if fc == 0:
                        w0 = [("TT", wrdy), ("TT", xqf_rdy),
                              ("VE", st.get(("gub", tt % 2), 0))]
                    pe(lambda e, o=gps, a=lh, b=grh, s=(fc == 0):
                       e.matmul(o, a, b, start=s, stop=False), waits=w0)
                    pe(lambda e, o=gps, a=lh, b=grl:
                       e.matmul(o, a, b, start=False, stop=False))
                    pe(lambda e, o=gps, a=ll, b=grh:
                       e.matmul(o, a, b, start=False, stop=(fc == FC - 1)))
                    pe(lambda e, o=ups, a=lh, b=urh, s=(fc == 0):
                       e.matmul(o, a, b, start=s, stop=False))
                    pe(lambda e, o=ups, a=lh, b=url:
                       e.matmul(o, a, b, start=False, stop=False))
                    petok = pe(lambda e, o=ups, a=ll, b=urh,
                               q=(fc == FC - 1):
                               e.matmul(o, a, b, start=False, stop=q),
                               inc="PE" if fc == FC - 1 else None)
                st["gu_pe"] = petok
                st["xqf_pe"] = petok
                sil = a32[:, SIL:SIL + 256]
                at = ac(lambda e, o=sil, i=gps: e.activation(o, i, AF.Silu),
                        waits=[("PE", petok), ("VE", st.get("hbm_ve", 0))],
                        tag=f"silu{wg}.{tt}", dr=False)
                hb = a32[:, HB:HB + 256]
                vt = ve(lambda e, o=hb, a=sil, b=ups:
                        e.tensor_tensor(o, a, b, ALU.mult),
                        waits=[("AC", at)], dr=False)
                st[("gub", tt % 2)] = vt
                st["hbm_ve"] = vt
                if half == 0 and wg == 0 and tt == 0 and "t_p4sil" in DEBUG_TAPS:
                    dma("gp", ap["t_p4sil"], sil, "ST", [("VE", vt)], "p4sil")
                    dma("gp", ap["t_p4hb"], hb, "ST", [("VE", vt)])
                    dma("gp", ap["t_p4gwt"], a16[:, GWTH:GWTH + 4096], "ST",
                        [("VE", vt)])
                    dma("gp", ap["t_p4xqf"], xqf[:, :4096], "ST",
                        [("VE", vt)])
                    ptk = p.tok("ST")
                    for een in ("ve", "ac", "pe"):
                        p.pending[een].append(("ST", ptk))
                hq = a32[:, WDQ + D:WDQ + D + 256] if False else \
                    a32[:, SIL + 256:SIL + 512]
                hq = a32[:, HB + 256:HB + 512]
                vt = mxq_chain(hb, hq, 256, [("VE", vt)], f"hq{wg}.{tt}")
                hh_ = a16[:, HQB:HQB + 256]
                hl_ = a16[:, HQB + 256:HQB + 512]
                ve(lambda e, o=hh_, i=hq: e.tensor_copy(o, i),
                   waits=[("ST", st.get("hq_st", 0))])
                vt = ve(lambda e, o=hl_, a=hq, b=hh_:
                        e.tensor_tensor(o, a, b, ALU.subtract))
                jj = half * 8 + tt
                dma("gp", ap["hqscrh"][jj][:, wg * 256:(wg + 1) * 256],
                    hh_, "ST", [("VE", vt)], f"sphq{jj}.{wg}")
                st["hq_st"] = dma(
                    "gp", ap["hqscrl"][jj][:, wg * 256:(wg + 1) * 256],
                    hl_, "ST")

        gu_done = st.get("gu_pe", 0)
        for tt in range(8):
            jj = half * 8 + tt
            for s_i, sn, fmn in ((0, "hqscrh", "hqfmh"), (1, "hqscrl", "hqfml")):
                strip = a16[:, GWTH:GWTH + 8192]
                tws = dmat(strip.rearrange("q (c f) -> q c f", f=128),
                           ap[sn][jj].rearrange("q (c f) -> q c f", f=128),
                           [("ST", st.get("hq_st", 0)), ("PE", gu_done),
                            ("ST", st.get("hqfmsp", 0))],
                           f"hqfmT{jj}.{s_i}")
                st["hqfmsp"] = dma("gp", ap[fmn][jj], strip,
                                   "ST", [("TT", tws)], f"hqfmsp{jj}.{s_i}")

    emit_down(p, nc, T, sb, ap, dma, dmat, ve, ac, pe, tap, ln_chain,
              CONST, _ap_)


def emit_down(p, nc, T, sb, ap, dma, dmat, ve, ac, pe, tap, ln_chain,
              CONST, _ap_):
    a32, a16 = sb["a32"], sb["a16"]
    ps = sb["ps"]
    st = p.state
    p.barrier()
    LHS, RHS, PCS, TOC = 0, 4096, 8192, 9728
    DT, R1, X1L, LNS4 = 0, 8192, 8704, 10752

    for tc in range(4):
        for ocg in range(2):
            for hc in range(HFC):
                dbl = hc % 2
                lhs = a16[:, LHS + dbl * 2048: LHS + dbl * 2048 + 2048]
                rhs_h = a16[:, RHS + dbl * 1024: RHS + dbl * 1024 + 512]
                rhs_l = a16[:, RHS + dbl * 1024 + 512:
                            RHS + dbl * 1024 + 1024]
                wld = [("PE", st.get(("dslot", dbl), 0))]
                dma("sy", lhs[:, :1024].rearrange("q (c f) -> q c f", f=128),
                    bass.AP(T["dwfmh"], ocg * 8 * 128 * FF + hc * 128,
                            [[FF, 128], [128 * FF, 8], [1, 128]]),
                    "L4", wld, f"dlh{tc}.{ocg}.{hc}")
                dma("sy", lhs[:, 1024:2048].rearrange("q (c f) -> q c f", f=128),
                    bass.AP(T["dwfml"], ocg * 8 * 128 * FF + hc * 128,
                            [[FF, 128], [128 * FF, 8], [1, 128]]), "L4")
                dma("sy", rhs_h.rearrange("q (c f) -> q c f", f=128),
                    bass.AP(T["hqfmh"], tc * 4 * 128 * FF + hc * 128,
                            [[FF, 128], [128 * FF, 4], [1, 128]]),
                    "L4", wld, f"drh{tc}.{ocg}.{hc}")
                lw = dma("sy", rhs_l.rearrange("q (c f) -> q c f", f=128),
                         bass.AP(T["hqfml"], tc * 4 * 128 * FF + hc * 128,
                                 [[FF, 128], [128 * FF, 4], [1, 128]]), "L4")
                for oi in range(8):
                    bank = ps[oi][:, :]
                    lh = lhs[:, oi * 128:(oi + 1) * 128]
                    ll = lhs[:, 1024 + oi * 128: 1024 + (oi + 1) * 128]
                    w0 = [("L4", lw)]
                    if hc == 0:
                        w0.append(("VE", st.get(("dbank", oi), 0)))
                    pe(lambda e, o=bank, a=lh, b=rhs_h, s=(hc == 0):
                       e.matmul(o, a, b, start=s, stop=False), waits=w0,
                       tag=f"dn{tc}.{ocg}.{hc}.{oi}")
                    pe(lambda e, o=bank, a=lh, b=rhs_l:
                       e.matmul(o, a, b, start=False, stop=False))
                    petok = pe(lambda e, o=bank, a=ll, b=rhs_h,
                               q=(hc == HFC - 1):
                               e.matmul(o, a, b, start=False, stop=q),
                               inc="PE" if oi == 7 else None)
                st[("dslot", dbl)] = petok
            grp_pe = p.tok("PE")
            for oi in range(8):
                oc = ocg * 8 + oi
                slot = oc % 2
                bank = ps[oi][:, :]
                hi = a16[:, PCS:PCS + 512]
                r1 = a32[:, R1:R1 + 512]
                lo = a16[:, PCS + 512:PCS + 1024]
                lo2 = a16[:, PCS + 1024:PCS + 1536]
                ve(lambda e, o=hi, i=bank: e.tensor_copy(o, i),
                   waits=[("PE", grp_pe), ("ST", st.get("dpsp", 0))],
                   tag=f"d3w{tc}.{oc}")
                ve(lambda e, o=r1, a=bank, b=hi:
                   e.tensor_tensor(o, a, b, ALU.subtract))
                ve(lambda e, o=lo, i=r1: e.tensor_copy(o, i))
                vt = ve(lambda e, o=lo2, a=r1, b=lo:
                        e.tensor_tensor(o, a, b, ALU.subtract))
                st[("dbank", oi)] = vt
                spw = [("VE", vt), ("TT", st.get(("dpT", slot), 0))]
                for k, srcp in enumerate((hi, lo, lo2)):
                    dma("gp", ap["dpscr"][slot][k], srcp, "ST",
                        spw if k == 0 else (), f"spdp{tc}.{oc}.{k}")
                for k, srcp in enumerate((hi, lo, lo2)):
                    dma("gp", ap["dpscr"][slot][k], srcp, "ST", (),
                        f"spdp2{tc}.{oc}.{k}")
                st["dpsp"] = p.tok("ST")
                ttok = 0
                for rep_ in range(2):
                    for k in range(3):
                        ttok = dmat(
                            a16[:, TOC + k * 512:TOC + (k + 1) * 512].rearrange(
                                "q (c f) -> q c f", f=128),
                            ap["dpscr"][slot][k].rearrange(
                                "q (c f) -> q c f", f=128),
                            ([("ST", st["dpsp"]),
                              ("VE", st.get("dacc", 0))]
                             if (k == 0 and rep_ == 0) else ()),
                            f"dT{tc}.{oc}.{k}.{rep_}")
                st[("dpT", slot)] = ttok
                for k in range(3):
                    srcp = _ap_(a16, TOC + k * 512, [[128, 4], [1, 128]])
                    dstp = _ap_(a32, DT + oc * 128, [[2048, 4], [1, 128]])
                    if k == 0:
                        st["dacc"] = ve(
                            lambda e, o=dstp, i=srcp: e.tensor_copy(o, i),
                            waits=[("TT", ttok)], tag=f"dacc{tc}.{oc}")
                    else:
                        st["dacc"] = ve(
                            lambda e, o=dstp, a=dstp, b=srcp:
                            e.tensor_tensor(o, a, b, ALU.add),
                            waits=[("TT", ttok)], dr=False)
        acc_done = p.tok("VE")
        for tt4 in range(4):
            j = tc * 4 + tt4
            dtt = a32[:, DT + tt4 * 2048: DT + (tt4 + 1) * 2048]
            dma("sy", a32[:, X1L:X1L + D], ap["x1scr"][j], "L4",
                [("ST", st.get("yst", 0))], f"x1l{j}")
            ldx = dma("sy", a32[:, X1L:X1L + D], ap["x1scr"][j], "L4",
                      (), f"x1lb{j}")
            vt = ve(lambda e, o=dtt, a=dtt, b=a32[:, X1L:X1L + D]:
                    e.tensor_tensor(o, a, b, ALU.add),
                    waits=[("L4", ldx), ("VE", acc_done)])
            vt = ln_chain(dtt, a32[:, LNS4:LNS4 + D], a32[:, LNS4:LNS4 + D],
                          [("VE", vt), ("ST", st.get("yst", 0))], f"ln3.{j}")
            tap(f"t_y_{j}", a32[:, LNS4:LNS4 + D], [("VE", vt)])
            st["yst"] = dma("gp", ap["y"][j * 128:(j + 1) * 128, :],
                            a32[:, LNS4:LNS4 + D], "ST", [("VE", vt)],
                            f"y{j}")
    for name in DEBUG_TAPS:
        if name.startswith("d_"):
            _, scr, idx = name.split("_")
            dma("gp", ap[name], ap[scr][int(idx)], "ST",
                [("ST", p.tok("ST"))], f"dump{name}")
    p.emit("gp", lambda e: e.memset(a16[:1, :1], 0),
           waits=[("ST", p.tok("ST"))], tag="final")


def kernel(L0_attn_X, L0_attn_ln_g, L0_attn_ln_b, L0_attn_Wq, L0_attn_bq,
           L0_attn_Wk, L0_attn_bk, L0_attn_Wv, L0_attn_bv, L0_attn_Wo,
           L0_attn_bo, L0_ffn_gate_w, L0_ffn_up_w, L0_ffn_down_w):
    X = np.ascontiguousarray(np.asarray(L0_attn_X, dtype=np.float32))
    nc = build_nc()

    def b16(v):
        return np.ascontiguousarray(np.asarray(v, dtype=np.float32)
                                    .reshape(FC, 128).T)

    common = {
        "wq": np.ascontiguousarray(np.asarray(L0_attn_Wq, np.float32)),
        "wk": np.ascontiguousarray(np.asarray(L0_attn_Wk, np.float32)),
        "wv": np.ascontiguousarray(np.asarray(L0_attn_Wv, np.float32)),
        "wo": np.ascontiguousarray(np.asarray(L0_attn_Wo, np.float32)),
        "gw": np.ascontiguousarray(np.asarray(L0_ffn_gate_w, np.float32)),
        "uw": np.ascontiguousarray(np.asarray(L0_ffn_up_w, np.float32)),
        "dw": np.ascontiguousarray(np.asarray(L0_ffn_down_w, np.float32)),
        "lng": np.ascontiguousarray(np.asarray(L0_attn_ln_g, np.float32)
                                    .reshape(1, D)),
        "lnb": np.ascontiguousarray(np.asarray(L0_attn_ln_b, np.float32)
                                    .reshape(1, D)),
        "bo2": np.ascontiguousarray(np.asarray(L0_attn_bo, np.float32)
                                    .reshape(1, D)),
        "bq16": b16(L0_attn_bq), "bk16": b16(L0_attn_bk),
        "bv16": b16(L0_attn_bv),
        "onescol": np.ones((128, 1), dtype=ml_dtypes.bfloat16),
        "onesrow": np.ones((1, 128), dtype=ml_dtypes.bfloat16),
    }
    masks = {}
    for par in range(2):
        m = np.full((128, 4, 2, 128), NEG, dtype=np.float32)
        k = np.arange(128)[:, None]
        r = np.arange(128)[None, :]
        for i in range(4):
            for cq in range(2):
                if i < 2 * cq:
                    m[:, i, cq, :] = 0.0
                elif i == 2 * cq:
                    m[:, i, cq, :] = np.where(k <= r, 0.0, NEG)
                elif i == 2 * cq + 1 and par == 1:
                    m[:, i, cq, :] = 0.0
        masks[par] = np.ascontiguousarray(m.reshape(128, 4 * QT))

    in_maps = []
    for c in range(8):
        b, par = c // 2, c % 2
        xbm = X[b]
        if par == 1:
            perm = [i + 1 if i % 2 == 0 else i - 1 for i in range(32)]
            xbm = xbm.reshape(32, 128, D)[perm].reshape(S, D)
        m = dict(common)
        m["xb"] = np.ascontiguousarray(xbm)
        m["mask4"] = masks[par]
        in_maps.append(m)

    res = run_bass_kernel_spmd(nc, in_maps, core_ids=list(range(8)))
    out = np.empty((B, S, D), dtype=np.float32)
    for c in range(8):
        b, par = c // 2, c % 2
        yc = np.asarray(res.results[c]["y"]).reshape(NJ, 128, D)
        for j in range(NJ):
            out[b, (2 * j + par) * 128:(2 * j + par + 1) * 128, :] = yc[j]
    return out


# version 06
# speedup vs baseline: 1.0025x; 1.0025x over previous
import contextlib
import numpy as np
import ml_dtypes
import concourse.bass as bass
import concourse.mybir as mybir
from concourse.bass_utils import run_bass_kernel_spmd

F32 = mybir.dt.float32
BF16 = mybir.dt.bfloat16
FP8 = mybir.dt.float8e4
ALU = mybir.AluOpType
AF = mybir.ActivationFunctionType
AX = mybir.AxisListType.X

B, S, D, NH, HD, FF = 4, 4096, 2048, 16, 128, 8192
FC = D // 128
OWN = 2048
NJ = OWN // 128
QT = 256
NQT = OWN // QT
HFC = FF // 128
EPS = 1e-5
NEG = -1.0e9
C448 = float(np.float32(1.0) / np.float32(448.0))
EPS30 = float(np.float32(1e-30))
RSQ = float(np.float32(1.0) / np.float32(np.sqrt(np.float32(128.0))))
INV_D = float(np.float32(1.0) / np.float32(D))

ENGS = ("pe", "ve", "ac", "sy", "gp")
DEBUG_TAPS = {}


class Prog:
    def __init__(self):
        self.streams = {e: [] for e in ENGS}
        self.cnt = {}
        self.pending = {e: [] for e in ENGS}
        self.state = {}

    def emit(self, eng, fn, waits=(), inc=None, n=1, tag=""):
        w = [(s, v) for (s, v) in waits if v > 0]
        if self.pending[eng]:
            w = self.pending[eng] + w
            self.pending[eng] = []
        if inc is not None:
            self.cnt[inc] = self.cnt.get(inc, 0) + n
        self.streams[eng].append((w, fn, inc, n, tag))
        return self.cnt.get(inc, 0)

    def tok(self, sem):
        return self.cnt.get(sem, 0)

    def barrier(self):
        snap = [(s, v) for s, v in self.cnt.items() if v > 0]
        for e in ENGS:
            self.pending[e] = list(snap)

    def check(self):
        ptr = {e: 0 for e in ENGS}
        val = {s: 0 for s in self.cnt}
        progress = True
        while progress:
            progress = False
            for e in ENGS:
                while ptr[e] < len(self.streams[e]):
                    w, fn, inc, n, tag = self.streams[e][ptr[e]]
                    if all(val.get(s, 0) >= v for (s, v) in w):
                        if inc is not None:
                            val[inc] = val.get(inc, 0) + n
                        ptr[e] += 1
                        progress = True
                    else:
                        break
        stuck = {e: ptr[e] for e in ENGS if ptr[e] < len(self.streams[e])}
        if stuck:
            det = []
            for e, i in stuck.items():
                w, fn, inc, n, tag = self.streams[e][i]
                bad = [(s, v, val.get(s, 0)) for (s, v) in w
                       if val.get(s, 0) < v]
                det.append(f"{e}[{i}/{len(self.streams[e])}] tag={tag} "
                           f"needs={bad}")
            raise RuntimeError("schedule deadlock: " + "; ".join(det))
        big = {s: v for s, v in val.items() if v > 60000}
        if big:
            raise RuntimeError(f"semaphore totals too large: {big}")


def _ap(t, off, dims):
    return bass.AP(t, off, [[t.shape[1], 128]] + [list(d) for d in dims])


def build_nc():
    nc = bass.Bass("TRN2", target_bir_lowering=False)
    T = {}
    T["xb"] = nc.dram_tensor("xb", [S, D], F32, kind="ExternalInput")
    for n, sh in (("wq", [D, D]), ("wk", [D, D]), ("wv", [D, D]),
                  ("wo", [D, D]), ("gw", [FF, D]), ("uw", [FF, D]),
                  ("dw", [D, FF]), ("lng", [1, D]), ("lnb", [1, D]),
                  ("bo2", [1, D]), ("bq16", [128, FC]), ("bk16", [128, FC]),
                  ("bv16", [128, FC]), ("mask4", [128, 4 * QT])):
        T[n] = nc.dram_tensor(n, sh, F32, kind="ExternalInput")
    T["onescol"] = nc.dram_tensor("onescol", [128, 1], BF16,
                                  kind="ExternalInput")
    T["onesrow"] = nc.dram_tensor("onesrow", [1, 128], BF16,
                                  kind="ExternalInput")
    T["y"] = nc.dram_tensor("y", [OWN, D], F32, kind="ExternalOutput")
    for n, sh in (("kscrh", [NH, 128, S]), ("kscrl", [NH, 128, S]),
                  ("vscrh", [NH, 128, S]), ("vscrl", [NH, 128, S]),
                  ("qscrh", [NH, 128, OWN]), ("qscrl", [NH, 128, OWN]),
                  ("ascrh", [NQT, 2, 128, NH * 128]), ("ascrl", [NQT, 2, 128, NH * 128]),
                  ("woscrh", [FC, 128, D]), ("woscrl", [FC, 128, D]),
                  ("hqscrh", [NJ, 128, FF]), ("hqscrl", [NJ, 128, FF]),
                  ):
        T[n] = nc.dram_tensor(n, sh, BF16)
    T["x1scr"] = nc.dram_tensor("x1scr", [NJ, 128, D], F32)
    for n, sh in (("xnscrh", [32, 128, D]), ("xnscrl", [32, 128, D]),
                  ("xqscrh", [NJ, 128, D]), ("xqscrl", [NJ, 128, D]),
                  ("gwsh", [FF, D]), ("gwsl", [FF, D]),
                  ("uwsh", [FF, D]), ("uwsl", [FF, D]),
                  ("dwsh", [D, FF]), ("dwsl", [D, FF]),
                  ("dwfmh", [FC, 128, FF]), ("dwfml", [FC, 128, FF]),
                  ("hqfmh", [NJ, 128, FF]), ("hqfml", [NJ, 128, FF]),
                  ("dpscr", [2, 3, 128, 512])):
        T[n] = nc.dram_tensor(n, sh, BF16)
    for name, (shape, dt) in DEBUG_TAPS.items():
        T[name] = nc.dram_tensor(name, list(shape), dt,
                                 kind="ExternalOutput")

    p = Prog()
    es = contextlib.ExitStack()
    with es:
        sb = {}
        sb["a32"] = es.enter_context(nc.sbuf_tensor("a32", [128, 14336], F32))
        sb["a16"] = es.enter_context(nc.sbuf_tensor("a16", [128, 24576], BF16))
        sb["xqf"] = es.enter_context(nc.sbuf_tensor("xqf", [128, 32768], BF16))
        sb["cmsk"] = es.enter_context(nc.sbuf_tensor("cmsk", [128, 4 * QT], F32))
        sb["cbor"] = es.enter_context(nc.sbuf_tensor("cbor", [128, D], F32))
        sb["cgr"] = es.enter_context(nc.sbuf_tensor("cgr", [128, D], F32))
        sb["cbr"] = es.enter_context(nc.sbuf_tensor("cbr", [128, D], F32))
        sb["cbq"] = es.enter_context(nc.sbuf_tensor("cbq", [128, FC], F32))
        sb["cbk"] = es.enter_context(nc.sbuf_tensor("cbk", [128, FC], F32))
        sb["cbv"] = es.enter_context(nc.sbuf_tensor("cbv", [128, FC], F32))
        sb["coc"] = es.enter_context(nc.sbuf_tensor("coc", [128, 1], BF16))
        sb["cor"] = es.enter_context(nc.sbuf_tensor("cor", [1, 128], BF16))
        sb["stat"] = es.enter_context(nc.sbuf_tensor("stat", [128, 288], F32))
        sb["srow"] = es.enter_context(nc.sbuf_tensor("srow", [1, QT], F32))
        sb["srwb"] = es.enter_context(nc.sbuf_tensor("srwb", [1, 2 * QT], BF16))
        sb["q8t"] = es.enter_context(nc.sbuf_tensor("q8t", [128, 2048], FP8))
        sb["ps"] = [es.enter_context(nc.psum_tensor(f"ps{i}", [128, 512], F32))
                    for i in range(8)]
        sem_names = ("LD", "TT", "ST", "PE", "VE", "AC", "LW", "LR",
                     "L3", "L4", "LX")
        sems = {s: es.enter_context(nc.semaphore(s)) for s in sem_names}

        emit_all(p, nc, T, sb)
        p.check()

        blk = es.enter_context(nc.Block())
        handles = dict(pe=blk.tensor, ve=blk.vector, ac=blk.scalar,
                       sy=blk.sync, gp=blk.gpsimd)
        for e in ENGS:
            stream = p.streams[e]

            def replay(eng, stream=stream):
                last = {}
                for w, fn, inc, n, tag in stream:
                    for s, v in w:
                        if last.get(s, -1) < v:
                            eng.wait_ge(sems[s], v)
                            last[s] = v
                    ins = fn(eng)
                    if inc is not None:
                        ins.then_inc(sems[inc], n)
            handles[e](replay)
    return nc


def emit_all(p, nc, T, sb):
    a32, a16, xqf = sb["a32"], sb["a16"], sb["xqf"]
    ps = sb["ps"]
    stat, srow, srwb, q8t = sb["stat"], sb["srow"], sb["srwb"], sb["q8t"]
    ap = {n: h.ap() for n, h in T.items()}
    st = p.state

    def dma(eng, out, in_, sem, waits=(), tag=""):
        p.emit(eng, lambda e, o=out, i=in_: e.dma_start(out=o, in_=i),
               waits=waits, inc=sem, n=16, tag=tag)
        return p.tok(sem)

    def dmat(out, in_, waits=(), tag=""):
        p.emit("ac", lambda e, o=out, i=in_:
               e.dma_start(out=o, in_=i, transpose=True),
               waits=waits, inc="TT", n=16, tag=tag)
        return p.tok("TT")

    def ve(fn, waits=(), tag="", dr=True):
        if dr:
            p.emit("ve", fn, waits=waits, tag=tag)
            p.emit("ve", lambda e: e.drain(fusable=True), inc="VE",
                   tag=tag + ".d")
        else:
            p.emit("ve", fn, waits=waits, inc="VE", tag=tag)
        return p.tok("VE")

    def ac(fn, waits=(), tag="", dr=True):
        if dr:
            p.emit("ac", fn, waits=waits, tag=tag)
            p.emit("ac", lambda e: e.drain(fusable=True), inc="AC",
                   tag=tag + ".d")
        else:
            p.emit("ac", fn, waits=waits, inc="AC", tag=tag)
        return p.tok("AC")

    def pe(fn, waits=(), inc=None, tag=""):
        p.emit("pe", fn, waits=waits, inc=inc, tag=tag)
        return p.tok("PE")

    def tap(name, src, waits, tag="tap"):
        if name in DEBUG_TAPS:
            dma("gp", ap[name], src, "ST", waits, tag)

    dma("sy", sb["cmsk"][:, :], ap["mask4"], "LD")
    dma("sy", sb["cbor"][:, :], ap["bo2"].broadcast_to([128, D]), "LD")
    dma("sy", sb["cgr"][:, :], ap["lng"].broadcast_to([128, D]), "LD")
    dma("sy", sb["cbr"][:, :], ap["lnb"].broadcast_to([128, D]), "LD")
    dma("sy", sb["cbq"][:, :], ap["bq16"], "LD")
    dma("sy", sb["cbk"][:, :], ap["bk16"], "LD")
    dma("sy", sb["cbv"][:, :], ap["bv16"], "LD")
    dma("sy", sb["coc"][:, :], ap["onescol"], "LD")
    CONST = dma("sy", sb["cor"][:, :], ap["onesrow"], "LD")
    tap("t_cgr", sb["cgr"][:, :], [("LD", CONST)])

    def ln_chain(src_ap, dump_ap, out_ap, pre_waits, tag):
        mus, ngm, ve_, rec, rsq, ssq = (stat[:, i:i + 1] for i in range(6))
        w0 = list(pre_waits) + [("AC", st.get("ln_ac", 0))]
        ve(lambda e, i=src_ap, o=mus: e.reduce_sum(o, i, AX),
           waits=w0, tag=tag + ".sum")
        vt = ve(lambda e, o=ngm, i=mus: e.tensor_scalar_mul(o, i, -INV_D))
        at = ac(lambda e, i=src_ap, o=dump_ap, b=ngm, s=ssq:
                e.activation(o, i, AF.Square, bias=b, scale=1.0,
                             accum_out=s),
                waits=[("VE", vt), ("LD", CONST)], tag=tag + ".sq")
        ve(lambda e, o=ve_, i=ssq: e.tensor_scalar(o, i, INV_D, EPS,
                                                   ALU.mult, ALU.add),
           waits=[("AC", at)])
        vt = ve(lambda e, o=rec, i=ve_: e.reciprocal(o, i))
        at = ac(lambda e, o=rsq, i=rec: e.activation(o, i, AF.Sqrt),
                waits=[("VE", vt)])
        st["ln_ac"] = at
        r_ = ve(lambda e, o=out_ap, i=src_ap, s1=ngm, s2=rsq:
                e.tensor_scalar(o, i, s1, s2, ALU.add, ALU.mult),
                waits=[("AC", at)], tag=tag + ".nrm")
        tap("t_stat_" + tag, stat[:, :8], [("VE", r_)])
        return r_

    def mxq_chain(src_ap, out_ap, n, pre_waits, tag):
        nb = n // 32
        am = stat[:, 8:8 + nb]
        s_ = stat[:, 8 + nb:8 + 2 * nb]
        rs2 = stat[:, 8 + 2 * nb:8 + 3 * nb]
        s2 = stat[:, 8 + 3 * nb:8 + 4 * nb]
        q8 = q8t[:, :n]
        ve(lambda e, i=src_ap, o=am: e.tensor_reduce(
            o.rearrange("q (b o) -> q b o", o=1),
            i.rearrange("q (b i) -> q b i", i=32),
            AX, ALU.max, apply_absolute_value=True),
            waits=pre_waits, tag=tag + ".amax")
        ve(lambda e, o=s_, i=am: e.tensor_scalar(o, i, C448, EPS30,
                                                 ALU.mult, ALU.max))
        ve(lambda e, o=rs2, i=s_: e.reciprocal(o, i))
        ve(lambda e, o=rs2, i=rs2: e.tensor_scalar_mul(o, i, 0.5))
        ve(lambda e, o=s2, i=s_: e.tensor_scalar_mul(o, i, 2.0))
        ve(lambda e, o=q8, i=src_ap, s=rs2, b=nb: e.tensor_tensor(
            o.rearrange("q (k i) -> q k i", i=32),
            i.rearrange("q (k i) -> q k i", i=32),
            s.unsqueeze(2).broadcast_to([128, b, 32]), ALU.mult),
            tag=tag + ".q8")
        return ve(lambda e, o=out_ap, i=q8, s=s2, b=nb: e.tensor_tensor(
            o.rearrange("q (k i) -> q k i", i=32),
            i.rearrange("q (k i) -> q k i", i=32),
            s.unsqueeze(2).broadcast_to([128, b, 32]), ALU.mult),
            tag=tag + ".deq")

    XRAW, LNS, XN, WF, EV = 0, 4096, 6144, 8192, 12288
    XNH, XNL, WH, WL, EVH, EVL = 0, 2048, 4096, 8192, 12288, 14336
    pan_pe = 0
    for pan in range(4):
        tt_tok = {}
        for tl in range(8):
            row0 = pan * 1024 + tl * 128
            slot = tl % 2
            xr = a32[:, XRAW + slot * D: XRAW + slot * D + D]
            ldt = dma("sy", xr, ap["xb"][row0:row0 + 128, :], "LD",
                      [("VE", st.get(("xr", slot), 0))], f"x{pan}.{tl}")
            tap(f"t_xr_{pan}_{tl}", xr, [("LD", ldt)])
            nt = ln_chain(xr, a32[:, LNS:LNS + D], a32[:, XN:XN + D],
                          [("LD", ldt)], f"ln1.{pan}.{tl}")
            xn = a32[:, XN:XN + D]
            ve(lambda e, o=xn, i=xn, c=sb["cgr"][:, :]:
               e.tensor_tensor(o, i, c, ALU.mult))
            ve(lambda e, o=xn, i=xn, c=sb["cbr"][:, :]:
               e.tensor_tensor(o, i, c, ALU.add))
            xh = a16[:, XNH:XNH + D]
            xl = a16[:, XNL:XNL + D]
            ve(lambda e, o=xh, i=xn: e.tensor_copy(o, i),
               waits=[("ST", st.get("xnsp", 0))])
            vt = ve(lambda e, o=xl, a=xn, b=xh:
                    e.tensor_tensor(o, a, b, ALU.subtract))
            st[("xr", slot)] = vt
            j32 = pan * 8 + tl
            dma("gp", ap["xnscrh"][j32], xh, "ST", [("VE", vt)], f"spxn{j32}")
            xnsp = dma("gp", ap["xnscrl"][j32], xl, "ST")
            st["xnsp"] = xnsp
            if pan == 0 and tl == 0 and "t_pin_xn" in DEBUG_TAPS:
                dma("gp", ap["t_pin_xn"], xn, "ST", [("VE", vt)], "pinxn")
                dma("gp", ap["t_pin_xh"], xh, "ST", [("VE", vt)], "pinxh")
                dma("gp", ap["t_pin_st"], stat[:, :8], "ST", [("VE", vt)])
                pintok = p.tok("ST")
                p.pending["ve"].append(("ST", pintok))
                p.pending["ac"].append(("ST", pintok))
            tap(f"t_xn_{pan}_{tl}", xn, [("VE", vt)])
            for s, scrn in ((0, "xnscrh"), (1, "xnscrl")):
                dst = _ap(xqf, s * 16384 + tl * 128, [[1024, FC], [1, 128]])
                dmat(dst, ap[scrn][j32].rearrange("q (c f) -> q c f", f=128),
                     [("ST", xnsp), ("PE", pan_pe)], f"xnfT{pan}.{tl}.{s}")
            tt_tok[tl] = p.tok("TT")
        xnf_rdy = p.tok("TT")
        if pan == 0:
            tap("t_xnfh", xqf[:, :16384], [("TT", xnf_rdy)])
            tap("t_xnfl", xqf[:, 16384:], [("TT", xnf_rdy)])

        for wn, bias_t, dsth, dstl, scale, is_q in (
                ("wk", "cbk", "kscrh", "kscrl", None, False),
                ("wv", "cbv", "vscrh", "vscrl", None, False),
                ("wq", "cbq", "qscrh", "qscrl", RSQ, True)):
            for oc in range(FC):
                db = oc % 2
                wf = a32[:, WF + db * D: WF + db * D + D]
                wh = a16[:, WH + db * D: WH + db * D + D]
                wl = a16[:, WL + db * D: WL + db * D + D]
                wsrc = bass.AP(T[wn], oc * 128,
                               [[D, 128], [128 * D, FC], [1, 128]])
                lw = dma("sy", wf.rearrange("q (c f) -> q c f", f=128), wsrc,
                         "LW", [("VE", st.get(("wf", db), 0))], f"{wn}{oc}")
                ve(lambda e, o=wh, i=wf: e.tensor_copy(o, i),
                   waits=[("LW", lw), ("PE", st.get(("whl", db), 0))])
                wtok = ve(lambda e, o=wl, a=wf, b=wh:
                          e.tensor_tensor(o, a, b, ALU.subtract))
                st[("wf", db)] = wtok
                for hh in range(1 if is_q else 2):
                    bk = (oc * 2 + hh) % 4
                    bank = ps[bk][:, :]
                    for fc in range(FC):
                        if is_q:
                            rh = _ap(xqf, fc * 1024, [[256, 4], [1, 128]])
                            rl = _ap(xqf, 16384 + fc * 1024,
                                     [[256, 4], [1, 128]])
                        else:
                            off = fc * 1024 + hh * 512
                            rh = xqf[:, off:off + 512]
                            rl = xqf[:, 16384 + off:16384 + off + 512]
                        lh = wh[:, fc * 128:(fc + 1) * 128]
                        ll = wl[:, fc * 128:(fc + 1) * 128]
                        w0 = ([("VE", wtok), ("TT", xnf_rdy),
                               ("VE", st.get(("bank", bk), 0))]
                              if fc == 0 else ())
                        pe(lambda e, o=bank, a=lh, b=rh, s=(fc == 0):
                           e.matmul(o, a, b, start=s, stop=False), waits=w0)
                        pe(lambda e, o=bank, a=lh, b=rl:
                           e.matmul(o, a, b, start=False, stop=False))
                        petok = pe(lambda e, o=bank, a=ll, b=rh,
                                   q=(fc == FC - 1):
                                   e.matmul(o, a, b, start=False, stop=q),
                                   inc="PE" if fc == FC - 1 else None)
                    st[("whl", db)] = petok
                    eb = a32[:, EV + db * 512: EV + db * 512 + 512]
                    eh = a16[:, EVH + db * 512: EVH + db * 512 + 512]
                    el = a16[:, EVL + db * 512: EVL + db * 512 + 512]
                    bias = sb[bias_t][:, oc:oc + 1]
                    ww = [("PE", petok), ("ST", st.get(("ev", db), 0))]
                    if scale is None:
                        st[("bank", bk)] = ve(
                            lambda e, o=eb, i=bank, b=bias:
                            e.tensor_scalar(o, i, b, None, ALU.add),
                            waits=ww, tag=f"ev{wn}{oc}", dr=False)
                    else:
                        st[("bank", bk)] = ve(
                            lambda e, o=eb, i=bank, b=bias:
                            e.tensor_scalar(o, i, b, scale, ALU.add,
                                            ALU.mult),
                            waits=ww, tag=f"ev{wn}{oc}", dr=False)
                    ve(lambda e, o=eh, i=eb: e.tensor_copy(o, i))
                    vt = ve(lambda e, o=el, a=eb, b=eh:
                            e.tensor_tensor(o, a, b, ALU.subtract))
                    col0 = pan * 512 if is_q else pan * 1024 + hh * 512
                    for src, dn in ((eh, dsth), (el, dstl)):
                        dma("gp", ap[dn][oc][:, col0:col0 + 512], src, "ST",
                            [("VE", vt)], f"sp{wn}{oc}")
                    st[("ev", db)] = p.tok("ST")
        pan_pe = p.tok("PE")

    P1ST = p.tok("ST")
    RGH, RGL, SHL, AVHL, DWB = 8192, 9216, 10240, 10752, 11264
    RGF, SAC, BSB, AVN, DWF, DWD, WOF = 0, 1024, 1280, 1536, 2048, 6144, 8192

    def load_head(h):
        slot = h % 2
        base = slot * 16384
        w = [("PE", st.get(("kvq", slot), 0)), ("ST", P1ST)]
        dma("sy", xqf[:, base:base + 4096], ap["kscrh"][h], "LD", w, f"kh{h}")
        dma("sy", xqf[:, base + 4096:base + 8192], ap["kscrl"][h], "LD")
        dmat(xqf[:, base + 8192:base + 12288].rearrange(
            "q (c f) -> q c f", f=128),
             ap["vscrh"][h].rearrange("q (c f) -> q c f", f=128), w, f"vT{h}")
        dmat(xqf[:, base + 12288:base + 16384].rearrange(
            "q (c f) -> q c f", f=128),
             ap["vscrl"][h].rearrange("q (c f) -> q c f", f=128))
        dma("sy", a16[:, slot * 4096:slot * 4096 + 2048], ap["qscrh"][h],
            "LD", w, f"qh{h}")
        dma("sy", a16[:, slot * 4096 + 2048:slot * 4096 + 4096],
            ap["qscrl"][h], "LD")
        st[("kvqld", slot)] = (p.tok("LD"), p.tok("TT"))

    load_head(0)
    for h in range(NH):
        if h + 1 < NH:
            load_head(h + 1)
        slot = h % 2
        base = slot * 16384
        ldt, ttt = st[("kvqld", slot)]
        qh = a16[:, slot * 4096: slot * 4096 + 2048]
        ql = a16[:, slot * 4096 + 2048: slot * 4096 + 4096]
        for t in range(NQT):
            nkc = 4 * (t + 1)
            qhi = qh[:, t * QT:(t + 1) * QT]
            qlo = ql[:, t * QT:(t + 1) * QT]
            avb = 2 + (t % 2)
            av = ps[avb][:, :QT]
            def emit_av(kc, vt):
                r = kc % 4
                vh_ = xqf[:, base + 8192 + kc * 128:
                          base + 8192 + (kc + 1) * 128]
                vl_ = xqf[:, base + 12288 + kc * 128:
                          base + 12288 + (kc + 1) * 128]
                rh = a16[:, RGH + r * QT: RGH + (r + 1) * QT]
                rl = a16[:, RGL + r * QT: RGL + (r + 1) * QT]
                pe(lambda e, o=av, a=vh_, b=rh, s=(kc == 0):
                   e.matmul(o, a, b, start=s, stop=False),
                   waits=[("VE", vt)])
                pe(lambda e, o=av, a=vh_, b=rl:
                   e.matmul(o, a, b, start=False, stop=False))
                st[("ring", kc % 4)] = pe(
                    lambda e, o=av, a=vl_, b=rh, q=(kc == nkc - 1):
                    e.matmul(o, a, b, start=False, stop=q), inc="PE")

            pend = []
            for kc in range(nkc):
                sbk = kc % 2
                sbank = ps[sbk][:, :QT]
                r = kc % 4
                kh_ = xqf[:, base + kc * 128: base + (kc + 1) * 128]
                kl_ = xqf[:, base + 4096 + kc * 128: base + 4096 + (kc + 1) * 128]
                w0 = [("LD", ldt), ("TT", ttt),
                      ("AC", st.get(("sbank", sbk), 0))]
                pe(lambda e, o=sbank, a=kh_, b=qhi:
                   e.matmul(o, a, b, start=True, stop=False), waits=w0,
                   tag=f"sc{h}.{t}.{kc}")
                pe(lambda e, o=sbank, a=kh_, b=qlo:
                   e.matmul(o, a, b, start=False, stop=False))
                sct = pe(lambda e, o=sbank, a=kl_, b=qhi:
                         e.matmul(o, a, b, start=False, stop=True), inc="PE")
                aw = [("PE", sct)]
                if kc >= nkc - 4:
                    mi = kc - (nkc - 4)
                    vt = ve(lambda e, o=sbank,
                            m=sb["cmsk"][:, mi * QT:(mi + 1) * QT]:
                            e.tensor_tensor(o, o, m, ALU.add),
                            waits=[("PE", sct)], tag=f"msk{kc}")
                    aw = [("VE", vt)]
                rh = a16[:, RGH + r * QT: RGH + (r + 1) * QT]
                rl = a16[:, RGL + r * QT: RGL + (r + 1) * QT]
                rf = a32[:, RGF + r * QT: RGF + (r + 1) * QT]
                aw.append(("PE", st.get(("ring", r), 0)))
                ac(lambda e, o=rh, i=sbank: e.activation(o, i, AF.Exp),
                   waits=aw, tag=f"exp{h}.{t}.{kc}", dr=False)
                st[("sbank", sbk)] = ac(lambda e, o=rf, i=sbank:
                                        e.activation(o, i, AF.Exp), dr=False)
                atok = p.tok("AC")
                ve(lambda e, o=rl, a=rf, b=rh:
                   e.tensor_tensor(o, a, b, ALU.subtract),
                   waits=[("AC", atok)], dr=False)
                if kc == 0:
                    vt = ve(lambda e, o=a32[:, SAC:SAC + QT], i=rf:
                            e.tensor_copy(o, i), dr=False)
                else:
                    vt = ve(lambda e, o=a32[:, SAC:SAC + QT], i=rf:
                            e.tensor_tensor(o, o, i, ALU.add), dr=False)
                pend.append((kc, vt))
                if len(pend) > 2:
                    emit_av(*pend.pop(0))
            for item in pend:
                emit_av(*item)
            avpe = p.tok("PE")
            sh_ = a16[:, SHL:SHL + QT]
            sl_ = a16[:, SHL + QT:SHL + 2 * QT]
            ve(lambda e, o=sh_, i=a32[:, SAC:SAC + QT]: e.tensor_copy(o, i),
               waits=[("PE", st.get("shl_pe", 0))], dr=False)
            vt = ve(lambda e, o=sl_, a=a32[:, SAC:SAC + QT], b=sh_:
                    e.tensor_tensor(o, a, b, ALU.subtract))
            rowp = ps[4][:1, :QT]
            pe(lambda e, o=rowp, a=sb["coc"][:, :], b=sh_:
               e.matmul(o, a, b, start=True, stop=False),
               waits=[("VE", vt), ("VE", st.get("row_free", 0)),
                      ("LD", CONST)])
            rt = pe(lambda e, o=rowp, a=sb["coc"][:, :], b=sl_:
                    e.matmul(o, a, b, start=False, stop=True), inc="PE")
            st["shl_pe"] = rt
            vt = ve(lambda e, o=srow[:1, :QT], i=rowp: e.reciprocal(o, i),
                    waits=[("PE", rt)], tag=f"rcp{h}.{t}")
            st["row_free"] = vt
            ve(lambda e, o=srwb[:1, :QT], i=srow[:1, :QT]:
               e.tensor_copy(o, i), waits=[("PE", st.get("srwb_pe", 0))],
               dr=False)
            vt = ve(lambda e, o=srwb[:1, QT:2 * QT], a=srow[:1, :QT],
                    b=srwb[:1, :QT]: e.tensor_tensor(o, a, b, ALU.subtract))
            bp = ps[5][:, :QT]
            pe(lambda e, o=bp, a=sb["cor"][:1, :], b=srwb[:1, :QT]:
               e.matmul(o, a, b, start=True, stop=False),
               waits=[("VE", vt), ("VE", st.get("b_free", 0))])
            bt_ = pe(lambda e, o=bp, a=sb["cor"][:1, :],
                     b=srwb[:1, QT:2 * QT]:
                     e.matmul(o, a, b, start=False, stop=True), inc="PE")
            st["srwb_pe"] = bt_
            ve(lambda e, o=a32[:, BSB:BSB + QT], i=bp: e.tensor_copy(o, i),
               waits=[("PE", bt_)], dr=False)
            st["b_free"] = p.tok("VE")
            avn = a32[:, AVN:AVN + QT]
            vt = ve(lambda e, o=avn, a=av, b=a32[:, BSB:BSB + QT]:
                    e.tensor_tensor(o, a, b, ALU.mult),
                    waits=[("PE", avpe)], tag=f"avn{h}.{t}", dr=False)
            ah_ = a16[:, AVHL:AVHL + QT]
            al_ = a16[:, AVHL + QT:AVHL + 2 * QT]
            ve(lambda e, o=ah_, i=avn: e.tensor_copy(o, i),
               waits=[("ST", st.get("spav", 0))])
            vt = ve(lambda e, o=al_, a=avn, b=ah_:
                    e.tensor_tensor(o, a, b, ALU.subtract))
            tap(f"t_avn_{h}_{t}", avn, [("VE", vt)])
            dsth_ = bass.AP(T["ascrh"], t * (2 * 128 * NH * 128) + h * 128,
                            [[NH * 128, 128], [128 * NH * 128, 2], [1, 128]])
            dstl_ = bass.AP(T["ascrl"], t * (2 * 128 * NH * 128) + h * 128,
                            [[NH * 128, 128], [128 * NH * 128, 2], [1, 128]])
            dma("gp", dsth_, ah_.rearrange("q (c f) -> q c f", f=128), "ST",
                [("VE", vt)], f"spav{h}.{t}")
            st["spav"] = dma("gp", dstl_,
                             al_.rearrange("q (c f) -> q c f", f=128), "ST")
            if h == 0 and t == 0 and "t_p2vh" in DEBUG_TAPS:
                vt0 = p.tok("VE")
                dma("gp", ap["t_p2vh"], xqf[:, base + 8192:base + 12288],
                    "ST", [("VE", vt0)], "dbgvh")
                dma("gp", ap["t_p2rgf"], a32[:, RGF:RGF + 1024], "ST",
                    [("VE", vt0)])
                dma("gp", ap["t_p2sac"], a32[:, SAC:SAC + QT], "ST",
                    [("VE", vt0)])
                dma("gp", ap["t_p2bsb"], a32[:, BSB:BSB + QT], "ST",
                    [("VE", vt0)])
                dma("gp", ap["t_p2avn"], a32[:, AVN:AVN + QT], "ST",
                    [("VE", vt0)])
                dma("gp", ap["t_p2qh"], a16[:, slot * 4096:slot * 4096 + 2048],
                    "ST", [("VE", vt0)])
                st["p2dbg"] = p.tok("ST")
                p.pending["ve"].append(("ST", st["p2dbg"]))
                p.pending["ac"].append(("ST", st["p2dbg"]))
                p.pending["pe"].append(("ST", st["p2dbg"]))
        st[("kvq", slot)] = p.tok("PE")

        wof = a32[:, WOF:WOF + 2048]
        woh = a16[:, DWB:DWB + 2048]
        wol = a16[:, DWB + 2048:DWB + 4096]
        lw = dma("sy", wof, ap["wo"][h * 128:(h + 1) * 128, :], "LR",
                 [("VE", st.get("wof_ve", 0))], f"wo{h}")
        ve(lambda e, o=woh, i=wof: e.tensor_copy(o, i),
           waits=[("LW", 0), ("LR", lw), ("ST", st.get("wo_st", 0)),
                  ("TT", st.get("dwt", 0))])
        vt = ve(lambda e, o=wol, a=wof, b=woh:
                e.tensor_tensor(o, a, b, ALU.subtract))
        st["wof_ve"] = vt
        dma("gp", ap["woscrh"][h], woh, "ST", [("VE", vt)], f"spwo{h}")
        st["wo_st"] = dma("gp", ap["woscrl"][h], wol, "ST")
        for qtr in range(4):
            dwf = a32[:, DWF:DWF + 2048]
            dwd = a32[:, DWD:DWD + 2048]
            dh = a16[:, DWB + 4096:DWB + 6144]
            dl = a16[:, DWB + 6144:DWB + 8192]
            srcq = ap["dw"][h * 128:(h + 1) * 128, qtr * 2048:(qtr + 1) * 2048]
            lw = dma("sy", dwf, srcq, "LR", [("VE", st.get("dwf_ve", 0))],
                     f"dw{h}.{qtr}")
            st["dwf_ve"] = mxq_chain(dwf, dwd, 2048, [("LR", lw)],
                                     f"dwq{h}.{qtr}")
            ve(lambda e, o=dh, i=dwd: e.tensor_copy(o, i),
               waits=[("ST", st.get("dwsp", 0))])
            vt = ve(lambda e, o=dl, a=dwd, b=dh:
                    e.tensor_tensor(o, a, b, ALU.subtract))
            dma("gp", ap["dwsh"][h * 128:(h + 1) * 128,
                                 qtr * 2048:(qtr + 1) * 2048],
                dh, "ST", [("VE", vt)], f"spdw{h}")
            st["dwsp"] = dma("gp",
                             ap["dwsl"][h * 128:(h + 1) * 128,
                                        qtr * 2048:(qtr + 1) * 2048],
                             dl, "ST")
        for s_i, sn, fmn in ((0, "dwsh", "dwfmh"), (1, "dwsl", "dwfml")):
            strip = a16[:, DWB:DWB + 8192]
            tws = dmat(strip.rearrange("q (c f) -> q c f", f=128),
                       ap[sn][h * 128:(h + 1) * 128, :].rearrange(
                           "q (c f) -> q c f", f=128),
                       [("ST", st["dwsp"]), ("ST", st.get("dwfmsp", 0))],
                       f"dwfmT{h}.{s_i}")
            st["dwfmsp"] = dma("gp", ap[fmn][h], strip,
                               "ST", [("TT", tws)], f"dwfmsp{h}.{s_i}")

    st["p2_pe"] = p.tok("PE")
    emit_p34(p, nc, T, sb, ap, dma, dmat, ve, ac, pe, tap, ln_chain,
             mxq_chain, CONST, _ap)


def emit_p34(p, nc, T, sb, ap, dma, dmat, ve, ac, pe, tap, ln_chain,
             mxq_chain, CONST, _ap_):
    a32, a16, xqf = sb["a32"], sb["a16"], sb["xqf"]
    ps = sb["ps"]
    st = p.state
    p.barrier()

    OT, X1, LNS3, XQ = 0, 2048, 4096, 6144
    AVT, WOS, XQH = 0, 4096, 8192
    GWTH, GWTL, UWTH, UWTL, PRET, HQB, HQT = (0, 4096, 8192, 12288,
                                              16384, 20480, 21504)
    WFQ, WDQ, SIL, HB = 0, 2048, 4096, 4608

    st["xqf_pe"] = st.get("p2_pe", 0)
    for half in range(2):
        p.barrier()
        for tt in [0] + list(range(8)):
            j = half * 8 + tt
            w = [("PE", st.get("avt_pe", 0))]
            dma("sy", a16[:, AVT:AVT + 2048], ap["ascrh"][j // 2][j % 2],
                "L3", w, f"av{j}")
            lda = dma("sy", a16[:, AVT + 2048:AVT + 4096],
                      ap["ascrl"][j // 2][j % 2], "L3")
            otile = a32[:, OT:OT + D]
            for oc in range(4):
                bk = oc % 4
                bank = ps[bk][:, :]
                for fc in range(FC):
                    db = fc % 2
                    rh = a16[:, WOS + db * 1024: WOS + db * 1024 + 512]
                    rl = a16[:, WOS + db * 1024 + 512:
                             WOS + db * 1024 + 1024]
                    dma("sy", rh,
                        ap["woscrh"][fc][:, oc * 512:(oc + 1) * 512],
                        "LR", [("PE", st.get(("wos", db), 0))],
                        f"wo{j}.{oc}.{fc}")
                    lw = dma("sy", rl,
                             ap["woscrl"][fc][:, oc * 512:(oc + 1) * 512],
                             "LR")
                    lh = a16[:, AVT + fc * 128: AVT + (fc + 1) * 128]
                    ll = a16[:, AVT + 2048 + fc * 128:
                             AVT + 2048 + (fc + 1) * 128]
                    w0 = [("LR", lw), ("L3", lda)]
                    if fc == 0:
                        w0.append(("VE", st.get(("p3bank", bk), 0)))
                    pe(lambda e, o=bank, a=lh, b=rh, s=(fc == 0):
                       e.matmul(o, a, b, start=s, stop=False), waits=w0)
                    pe(lambda e, o=bank, a=lh, b=rl:
                       e.matmul(o, a, b, start=False, stop=False))
                    petok = pe(lambda e, o=bank, a=ll, b=rh,
                               q=(fc == FC - 1):
                               e.matmul(o, a, b, start=False, stop=q),
                               inc="PE")
                    st[("wos", db)] = petok
                st[("p3bank", bk)] = ve(
                    lambda e, o=otile[:, oc * 512:(oc + 1) * 512], i=bank:
                    e.tensor_copy(o, i), waits=[("PE", petok)],
                    tag=f"oev{j}.{oc}", dr=False)
            st["avt_pe"] = petok
            vt = ve(lambda e, o=otile, i=otile, c=sb["cbor"][:, :]:
                    e.tensor_tensor(o, i, c, ALU.add), dr=False)
            ldx = dma("sy", a32[:, LNS3:LNS3 + D],
                      ap["xb"][j * 256: j * 256 + 128, :], "LX",
                      [("AC", st.get("ln_ac", 0))], f"xo{j}")
            vt = ve(lambda e, o=otile, a=otile, b=a32[:, LNS3:LNS3 + D]:
                    e.tensor_tensor(o, a, b, ALU.add), waits=[("LX", ldx)])
            x1 = a32[:, X1:X1 + D]
            vt = ln_chain(otile, a32[:, LNS3:LNS3 + D], x1,
                          [("VE", vt), ("ST", st.get("x1st", 0))],
                          f"ln2.{j}")
            tap(f"t_x1_{j}", x1, [("VE", vt)])
            st["x1st"] = dma("gp", ap["x1scr"][j], x1, "ST", [("VE", vt)],
                             f"spx1{j}")
            xq = a32[:, XQ:XQ + D]
            vt = mxq_chain(x1, xq, D, [("VE", vt)], f"xq{j}")
            xh = a16[:, XQH:XQH + D]
            xl = a16[:, XQH + D:XQH + 2 * D]
            ve(lambda e, o=xh, i=xq: e.tensor_copy(o, i),
               waits=[("ST", st.get("xqsp", 0))])
            vt = ve(lambda e, o=xl, a=xq, b=xh:
                    e.tensor_tensor(o, a, b, ALU.subtract))
            dma("gp", ap["xqscrh"][j], xh, "ST", [("VE", vt)], f"spxq{j}")
            xqsp = dma("gp", ap["xqscrl"][j], xl, "ST")
            st["xqsp"] = xqsp
            for s, scrn in ((0, "xqscrh"), (1, "xqscrl")):
                dst = _ap_(xqf, s * 16384 + tt * 128, [[1024, FC], [1, 128]])
                dmat(dst, ap[scrn][j].rearrange("q (c f) -> q c f", f=128),
                     [("ST", xqsp), ("PE", st["xqf_pe"])], f"xqT{j}.{s}")
            st["xqT"] = p.tok("TT")
        xqf_rdy = p.tok("TT")

        p.barrier()
        for wg in range(32):
            tts = 0
            for wname, bh, bl, sh_n, sl_n in (
                    ("gw", GWTH, GWTL, "gwsh", "gwsl"),
                    ("uw", UWTH, UWTL, "uwsh", "uwsl")):
                for m in range(2):
                    r0 = wg * 256 + m * 128
                    if half == 0:
                        wfq = a32[:, WFQ:WFQ + D]
                        wdq = a32[:, WDQ:WDQ + D]
                        ph_ = a16[:, PRET:PRET + D]
                        pl_ = a16[:, PRET + D:PRET + 2 * D]
                        lw = dma("sy", wfq, ap[wname][r0:r0 + 128, :], "L4",
                                 [("VE", st.get("wfq_ve", 0))],
                                 f"{wname}{wg}.{m}")
                        st["wfq_ve"] = mxq_chain(wfq, wdq, D, [("L4", lw)],
                                                 f"{wname}q{wg}.{m}")
                        ve(lambda e, o=ph_, i=wdq: e.tensor_copy(o, i),
                           waits=[("ST", st.get("wssp", 0))])
                        vt = ve(lambda e, o=pl_, a=wdq, b=ph_:
                                e.tensor_tensor(o, a, b, ALU.subtract))
                        dma("gp", ap[sh_n][r0:r0 + 128, :], ph_, "ST",
                            [("VE", vt)], f"spw{wname}{wg}.{m}")
                        wssp = dma("gp", ap[sl_n][r0:r0 + 128, :], pl_, "ST")
                        st["wssp"] = wssp
                    else:
                        wssp = 0
                    for s_i, scrn, bbase in ((0, sh_n, bh), (1, sl_n, bl)):
                        srcap = bass.AP(T[scrn], r0 * D,
                                        [[D, 128], [128, FC], [1, 128]])
                        tts = dmat(
                            _ap_(a16, bbase + m * 128, [[256, FC], [1, 128]]),
                            srcap,
                            [("ST", wssp), ("PE", st.get("gu_pe", 0))],
                            f"{wname}T{wg}.{m}")
                    st["pretT"] = tts
            wrdy = tts
            for tt in range(8):
                gb, ub = ps[(tt % 2) * 2], ps[(tt % 2) * 2 + 1]
                gps, ups = gb[:, :256], ub[:, :256]
                for fc in range(FC):
                    lh = xqf[:, fc * 1024 + tt * 128:
                             fc * 1024 + (tt + 1) * 128]
                    ll = xqf[:, 16384 + fc * 1024 + tt * 128:
                             16384 + fc * 1024 + (tt + 1) * 128]
                    grh = a16[:, GWTH + fc * 256: GWTH + (fc + 1) * 256]
                    grl = a16[:, GWTL + fc * 256: GWTL + (fc + 1) * 256]
                    urh = a16[:, UWTH + fc * 256: UWTH + (fc + 1) * 256]
                    url = a16[:, UWTL + fc * 256: UWTL + (fc + 1) * 256]
                    w0 = []
                    if fc == 0:
                        w0 = [("TT", wrdy), ("TT", xqf_rdy),
                              ("VE", st.get(("gub", tt % 2), 0))]
                    pe(lambda e, o=gps, a=lh, b=grh, s=(fc == 0):
                       e.matmul(o, a, b, start=s, stop=False), waits=w0)
                    pe(lambda e, o=gps, a=lh, b=grl:
                       e.matmul(o, a, b, start=False, stop=False))
                    pe(lambda e, o=gps, a=ll, b=grh:
                       e.matmul(o, a, b, start=False, stop=(fc == FC - 1)))
                    pe(lambda e, o=ups, a=lh, b=urh, s=(fc == 0):
                       e.matmul(o, a, b, start=s, stop=False))
                    pe(lambda e, o=ups, a=lh, b=url:
                       e.matmul(o, a, b, start=False, stop=False))
                    petok = pe(lambda e, o=ups, a=ll, b=urh,
                               q=(fc == FC - 1):
                               e.matmul(o, a, b, start=False, stop=q),
                               inc="PE" if fc == FC - 1 else None)
                st["gu_pe"] = petok
                st["xqf_pe"] = petok
                sil = a32[:, SIL:SIL + 256]
                at = ac(lambda e, o=sil, i=gps: e.activation(o, i, AF.Silu),
                        waits=[("PE", petok), ("VE", st.get("hbm_ve", 0))],
                        tag=f"silu{wg}.{tt}", dr=False)
                hb = a32[:, HB:HB + 256]
                vt = ve(lambda e, o=hb, a=sil, b=ups:
                        e.tensor_tensor(o, a, b, ALU.mult),
                        waits=[("AC", at)], dr=False)
                st[("gub", tt % 2)] = vt
                st["hbm_ve"] = vt
                if half == 0 and wg == 0 and tt == 0 and "t_p4sil" in DEBUG_TAPS:
                    dma("gp", ap["t_p4sil"], sil, "ST", [("VE", vt)], "p4sil")
                    dma("gp", ap["t_p4hb"], hb, "ST", [("VE", vt)])
                    dma("gp", ap["t_p4gwt"], a16[:, GWTH:GWTH + 4096], "ST",
                        [("VE", vt)])
                    dma("gp", ap["t_p4xqf"], xqf[:, :4096], "ST",
                        [("VE", vt)])
                    ptk = p.tok("ST")
                    for een in ("ve", "ac", "pe"):
                        p.pending[een].append(("ST", ptk))
                hq = a32[:, WDQ + D:WDQ + D + 256] if False else \
                    a32[:, SIL + 256:SIL + 512]
                hq = a32[:, HB + 256:HB + 512]
                vt = mxq_chain(hb, hq, 256, [("VE", vt)], f"hq{wg}.{tt}")
                hh_ = a16[:, HQB:HQB + 256]
                hl_ = a16[:, HQB + 256:HQB + 512]
                ve(lambda e, o=hh_, i=hq: e.tensor_copy(o, i),
                   waits=[("ST", st.get("hq_st", 0))])
                vt = ve(lambda e, o=hl_, a=hq, b=hh_:
                        e.tensor_tensor(o, a, b, ALU.subtract))
                jj = half * 8 + tt
                dma("gp", ap["hqscrh"][jj][:, wg * 256:(wg + 1) * 256],
                    hh_, "ST", [("VE", vt)], f"sphq{jj}.{wg}")
                st["hq_st"] = dma(
                    "gp", ap["hqscrl"][jj][:, wg * 256:(wg + 1) * 256],
                    hl_, "ST")

        gu_done = st.get("gu_pe", 0)
        for tt in range(8):
            jj = half * 8 + tt
            for s_i, sn, fmn in ((0, "hqscrh", "hqfmh"), (1, "hqscrl", "hqfml")):
                strip = a16[:, GWTH:GWTH + 8192]
                tws = dmat(strip.rearrange("q (c f) -> q c f", f=128),
                           ap[sn][jj].rearrange("q (c f) -> q c f", f=128),
                           [("ST", st.get("hq_st", 0)), ("PE", gu_done),
                            ("ST", st.get("hqfmsp", 0))],
                           f"hqfmT{jj}.{s_i}")
                st["hqfmsp"] = dma("gp", ap[fmn][jj], strip,
                                   "ST", [("TT", tws)], f"hqfmsp{jj}.{s_i}")

    emit_down(p, nc, T, sb, ap, dma, dmat, ve, ac, pe, tap, ln_chain,
              CONST, _ap_)


def emit_down(p, nc, T, sb, ap, dma, dmat, ve, ac, pe, tap, ln_chain,
              CONST, _ap_):
    a32, a16 = sb["a32"], sb["a16"]
    ps = sb["ps"]
    st = p.state
    p.barrier()
    LHS, RHS, PCS, TOC = 0, 4096, 8192, 9728
    DT, R1, X1L, LNS4 = 0, 8192, 8704, 10752

    for tc in range(4):
        for ocg in range(2):
            for hc in range(HFC):
                dbl = hc % 2
                lhs = a16[:, LHS + dbl * 2048: LHS + dbl * 2048 + 2048]
                rhs_h = a16[:, RHS + dbl * 1024: RHS + dbl * 1024 + 512]
                rhs_l = a16[:, RHS + dbl * 1024 + 512:
                            RHS + dbl * 1024 + 1024]
                wld = [("PE", st.get(("dslot", dbl), 0))]
                dma("sy", lhs[:, :1024].rearrange("q (c f) -> q c f", f=128),
                    bass.AP(T["dwfmh"], ocg * 8 * 128 * FF + hc * 128,
                            [[FF, 128], [128 * FF, 8], [1, 128]]),
                    "L4", wld, f"dlh{tc}.{ocg}.{hc}")
                dma("sy", lhs[:, 1024:2048].rearrange("q (c f) -> q c f", f=128),
                    bass.AP(T["dwfml"], ocg * 8 * 128 * FF + hc * 128,
                            [[FF, 128], [128 * FF, 8], [1, 128]]), "L4")
                dma("sy", rhs_h.rearrange("q (c f) -> q c f", f=128),
                    bass.AP(T["hqfmh"], tc * 4 * 128 * FF + hc * 128,
                            [[FF, 128], [128 * FF, 4], [1, 128]]),
                    "L4", wld, f"drh{tc}.{ocg}.{hc}")
                lw = dma("sy", rhs_l.rearrange("q (c f) -> q c f", f=128),
                         bass.AP(T["hqfml"], tc * 4 * 128 * FF + hc * 128,
                                 [[FF, 128], [128 * FF, 4], [1, 128]]), "L4")
                for oi in range(8):
                    bank = ps[oi][:, :]
                    lh = lhs[:, oi * 128:(oi + 1) * 128]
                    ll = lhs[:, 1024 + oi * 128: 1024 + (oi + 1) * 128]
                    w0 = [("L4", lw)]
                    if hc == 0:
                        w0.append(("VE", st.get(("dbank", oi), 0)))
                    pe(lambda e, o=bank, a=lh, b=rhs_h, s=(hc == 0):
                       e.matmul(o, a, b, start=s, stop=False), waits=w0,
                       tag=f"dn{tc}.{ocg}.{hc}.{oi}")
                    pe(lambda e, o=bank, a=lh, b=rhs_l:
                       e.matmul(o, a, b, start=False, stop=False))
                    petok = pe(lambda e, o=bank, a=ll, b=rhs_h,
                               q=(hc == HFC - 1):
                               e.matmul(o, a, b, start=False, stop=q),
                               inc="PE" if oi == 7 else None)
                st[("dslot", dbl)] = petok
            grp_pe = p.tok("PE")
            for oi in range(8):
                oc = ocg * 8 + oi
                slot = oc % 2
                bank = ps[oi][:, :]
                hi = a16[:, PCS:PCS + 512]
                r1 = a32[:, R1:R1 + 512]
                lo = a16[:, PCS + 512:PCS + 1024]
                lo2 = a16[:, PCS + 1024:PCS + 1536]
                ve(lambda e, o=hi, i=bank: e.tensor_copy(o, i),
                   waits=[("PE", grp_pe), ("ST", st.get("dpsp", 0))],
                   tag=f"d3w{tc}.{oc}")
                ve(lambda e, o=r1, a=bank, b=hi:
                   e.tensor_tensor(o, a, b, ALU.subtract))
                ve(lambda e, o=lo, i=r1: e.tensor_copy(o, i))
                vt = ve(lambda e, o=lo2, a=r1, b=lo:
                        e.tensor_tensor(o, a, b, ALU.subtract))
                st[("dbank", oi)] = vt
                spw = [("VE", vt), ("TT", st.get(("dpT", slot), 0))]
                for k, srcp in enumerate((hi, lo, lo2)):
                    dma("gp", ap["dpscr"][slot][k], srcp, "ST",
                        spw if k == 0 else (), f"spdp{tc}.{oc}.{k}")
                for k, srcp in enumerate((hi, lo, lo2)):
                    dma("gp", ap["dpscr"][slot][k], srcp, "ST", (),
                        f"spdp2{tc}.{oc}.{k}")
                st["dpsp"] = p.tok("ST")
                ttok = 0
                for rep_ in range(2):
                    for k in range(3):
                        ttok = dmat(
                            a16[:, TOC + k * 512:TOC + (k + 1) * 512].rearrange(
                                "q (c f) -> q c f", f=128),
                            ap["dpscr"][slot][k].rearrange(
                                "q (c f) -> q c f", f=128),
                            ([("ST", st["dpsp"]),
                              ("VE", st.get("dacc", 0))]
                             if (k == 0 and rep_ == 0) else ()),
                            f"dT{tc}.{oc}.{k}.{rep_}")
                st[("dpT", slot)] = ttok
                for k in range(3):
                    srcp = _ap_(a16, TOC + k * 512, [[128, 4], [1, 128]])
                    dstp = _ap_(a32, DT + oc * 128, [[2048, 4], [1, 128]])
                    if k == 0:
                        st["dacc"] = ve(
                            lambda e, o=dstp, i=srcp: e.tensor_copy(o, i),
                            waits=[("TT", ttok)], tag=f"dacc{tc}.{oc}")
                    else:
                        st["dacc"] = ve(
                            lambda e, o=dstp, a=dstp, b=srcp:
                            e.tensor_tensor(o, a, b, ALU.add),
                            waits=[("TT", ttok)], dr=False)
        acc_done = p.tok("VE")
        for tt4 in range(4):
            j = tc * 4 + tt4
            dtt = a32[:, DT + tt4 * 2048: DT + (tt4 + 1) * 2048]
            dma("sy", a32[:, X1L:X1L + D], ap["x1scr"][j], "L4",
                [("ST", st.get("yst", 0))], f"x1l{j}")
            ldx = dma("sy", a32[:, X1L:X1L + D], ap["x1scr"][j], "L4",
                      (), f"x1lb{j}")
            vt = ve(lambda e, o=dtt, a=dtt, b=a32[:, X1L:X1L + D]:
                    e.tensor_tensor(o, a, b, ALU.add),
                    waits=[("L4", ldx), ("VE", acc_done)])
            vt = ln_chain(dtt, a32[:, LNS4:LNS4 + D], a32[:, LNS4:LNS4 + D],
                          [("VE", vt), ("ST", st.get("yst", 0))], f"ln3.{j}")
            tap(f"t_y_{j}", a32[:, LNS4:LNS4 + D], [("VE", vt)])
            st["yst"] = dma("gp", ap["y"][j * 128:(j + 1) * 128, :],
                            a32[:, LNS4:LNS4 + D], "ST", [("VE", vt)],
                            f"y{j}")
    for name in DEBUG_TAPS:
        if name.startswith("d_"):
            _, scr, idx = name.split("_")
            dma("gp", ap[name], ap[scr][int(idx)], "ST",
                [("ST", p.tok("ST"))], f"dump{name}")
    p.emit("gp", lambda e: e.memset(a16[:1, :1], 0),
           waits=[("ST", p.tok("ST"))], tag="final")


def kernel(L0_attn_X, L0_attn_ln_g, L0_attn_ln_b, L0_attn_Wq, L0_attn_bq,
           L0_attn_Wk, L0_attn_bk, L0_attn_Wv, L0_attn_bv, L0_attn_Wo,
           L0_attn_bo, L0_ffn_gate_w, L0_ffn_up_w, L0_ffn_down_w):
    X = np.ascontiguousarray(np.asarray(L0_attn_X, dtype=np.float32))
    nc = build_nc()

    def b16(v):
        return np.ascontiguousarray(np.asarray(v, dtype=np.float32)
                                    .reshape(FC, 128).T)

    common = {
        "wq": np.ascontiguousarray(np.asarray(L0_attn_Wq, np.float32)),
        "wk": np.ascontiguousarray(np.asarray(L0_attn_Wk, np.float32)),
        "wv": np.ascontiguousarray(np.asarray(L0_attn_Wv, np.float32)),
        "wo": np.ascontiguousarray(np.asarray(L0_attn_Wo, np.float32)),
        "gw": np.ascontiguousarray(np.asarray(L0_ffn_gate_w, np.float32)),
        "uw": np.ascontiguousarray(np.asarray(L0_ffn_up_w, np.float32)),
        "dw": np.ascontiguousarray(np.asarray(L0_ffn_down_w, np.float32)),
        "lng": np.ascontiguousarray(np.asarray(L0_attn_ln_g, np.float32)
                                    .reshape(1, D)),
        "lnb": np.ascontiguousarray(np.asarray(L0_attn_ln_b, np.float32)
                                    .reshape(1, D)),
        "bo2": np.ascontiguousarray(np.asarray(L0_attn_bo, np.float32)
                                    .reshape(1, D)),
        "bq16": b16(L0_attn_bq), "bk16": b16(L0_attn_bk),
        "bv16": b16(L0_attn_bv),
        "onescol": np.ones((128, 1), dtype=ml_dtypes.bfloat16),
        "onesrow": np.ones((1, 128), dtype=ml_dtypes.bfloat16),
    }
    masks = {}
    for par in range(2):
        m = np.full((128, 4, 2, 128), NEG, dtype=np.float32)
        k = np.arange(128)[:, None]
        r = np.arange(128)[None, :]
        for i in range(4):
            for cq in range(2):
                if i < 2 * cq:
                    m[:, i, cq, :] = 0.0
                elif i == 2 * cq:
                    m[:, i, cq, :] = np.where(k <= r, 0.0, NEG)
                elif i == 2 * cq + 1 and par == 1:
                    m[:, i, cq, :] = 0.0
        masks[par] = np.ascontiguousarray(m.reshape(128, 4 * QT))

    in_maps = []
    for c in range(8):
        b, par = c // 2, c % 2
        xbm = X[b]
        if par == 1:
            perm = [i + 1 if i % 2 == 0 else i - 1 for i in range(32)]
            xbm = xbm.reshape(32, 128, D)[perm].reshape(S, D)
        m = dict(common)
        m["xb"] = np.ascontiguousarray(xbm)
        m["mask4"] = masks[par]
        in_maps.append(m)

    res = run_bass_kernel_spmd(nc, in_maps, core_ids=list(range(8)))
    out = np.empty((B, S, D), dtype=np.float32)
    for c in range(8):
        b, par = c // 2, c % 2
        yc = np.asarray(res.results[c]["y"]).reshape(NJ, 128, D)
        for j in range(NJ):
            out[b, (2 * j + par) * 128:(2 * j + par + 1) * 128, :] = yc[j]
    return out


# version 07
# speedup vs baseline: 1.0146x; 1.0121x over previous
import contextlib
import numpy as np
import ml_dtypes
import concourse.bass as bass
import concourse.mybir as mybir
from concourse.bass_utils import run_bass_kernel_spmd

F32 = mybir.dt.float32
BF16 = mybir.dt.bfloat16
FP8 = mybir.dt.float8e4
ALU = mybir.AluOpType
AF = mybir.ActivationFunctionType
AX = mybir.AxisListType.X

B, S, D, NH, HD, FF = 4, 4096, 2048, 16, 128, 8192
FC = D // 128
OWN = 2048
NJ = OWN // 128
QT = 256
NQT = OWN // QT
HFC = FF // 128
EPS = 1e-5
NEG = -1.0e9
C448 = float(np.float32(1.0) / np.float32(448.0))
EPS30 = float(np.float32(1e-30))
RSQ = float(np.float32(1.0) / np.float32(np.sqrt(np.float32(128.0))))
INV_D = float(np.float32(1.0) / np.float32(D))

ENGS = ("pe", "ve", "ac", "sy", "gp")
DEBUG_TAPS = {}


class Prog:
    def __init__(self):
        self.streams = {e: [] for e in ENGS}
        self.cnt = {}
        self.pending = {e: [] for e in ENGS}
        self.state = {}

    def emit(self, eng, fn, waits=(), inc=None, n=1, tag=""):
        w = [(s, v) for (s, v) in waits if v > 0]
        if self.pending[eng]:
            w = self.pending[eng] + w
            self.pending[eng] = []
        if inc is not None:
            self.cnt[inc] = self.cnt.get(inc, 0) + n
        self.streams[eng].append((w, fn, inc, n, tag))
        return self.cnt.get(inc, 0)

    def tok(self, sem):
        return self.cnt.get(sem, 0)

    def barrier(self):
        snap = [(s, v) for s, v in self.cnt.items() if v > 0]
        for e in ENGS:
            self.pending[e] = list(snap)

    def check(self):
        ptr = {e: 0 for e in ENGS}
        val = {s: 0 for s in self.cnt}
        progress = True
        while progress:
            progress = False
            for e in ENGS:
                while ptr[e] < len(self.streams[e]):
                    w, fn, inc, n, tag = self.streams[e][ptr[e]]
                    if all(val.get(s, 0) >= v for (s, v) in w):
                        if inc is not None:
                            val[inc] = val.get(inc, 0) + n
                        ptr[e] += 1
                        progress = True
                    else:
                        break
        stuck = {e: ptr[e] for e in ENGS if ptr[e] < len(self.streams[e])}
        if stuck:
            det = []
            for e, i in stuck.items():
                w, fn, inc, n, tag = self.streams[e][i]
                bad = [(s, v, val.get(s, 0)) for (s, v) in w
                       if val.get(s, 0) < v]
                det.append(f"{e}[{i}/{len(self.streams[e])}] tag={tag} "
                           f"needs={bad}")
            raise RuntimeError("schedule deadlock: " + "; ".join(det))
        big = {s: v for s, v in val.items() if v > 60000}
        if big:
            raise RuntimeError(f"semaphore totals too large: {big}")


def _ap(t, off, dims):
    return bass.AP(t, off, [[t.shape[1], 128]] + [list(d) for d in dims])


def build_nc():
    nc = bass.Bass("TRN2", target_bir_lowering=False)
    T = {}
    T["xb"] = nc.dram_tensor("xb", [S, D], F32, kind="ExternalInput")
    for n, sh in (("wq", [D, D]), ("wk", [D, D]), ("wv", [D, D]),
                  ("wo", [D, D]), ("gw", [FF, D]), ("uw", [FF, D]),
                  ("dw", [D, FF]), ("lng", [1, D]), ("lnb", [1, D]),
                  ("bo2", [1, D]), ("bq16", [128, FC]), ("bk16", [128, FC]),
                  ("bv16", [128, FC]), ("mask4", [128, 4 * QT])):
        T[n] = nc.dram_tensor(n, sh, F32, kind="ExternalInput")
    T["onescol"] = nc.dram_tensor("onescol", [128, 1], BF16,
                                  kind="ExternalInput")
    T["onesrow"] = nc.dram_tensor("onesrow", [1, 128], BF16,
                                  kind="ExternalInput")
    T["y"] = nc.dram_tensor("y", [OWN, D], F32, kind="ExternalOutput")
    for n, sh in (("kscrh", [NH, 128, S]), ("kscrl", [NH, 128, S]),
                  ("vscrh", [NH, 128, S]), ("vscrl", [NH, 128, S]),
                  ("qscrh", [NH, 128, OWN]), ("qscrl", [NH, 128, OWN]),
                  ("ascrh", [NQT, 2, 128, NH * 128]), ("ascrl", [NQT, 2, 128, NH * 128]),
                  ("woscrh", [FC, 128, D]), ("woscrl", [FC, 128, D]),
                  ("hqscrh", [NJ, 128, FF]), ("hqscrl", [NJ, 128, FF]),
                  ):
        T[n] = nc.dram_tensor(n, sh, BF16)
    T["x1scr"] = nc.dram_tensor("x1scr", [NJ, 128, D], F32)
    for n, sh in (("xnscrh", [32, 128, D]), ("xnscrl", [32, 128, D]),
                  ("xqscrh", [NJ, 128, D]), ("xqscrl", [NJ, 128, D]),
                  ("gwsh", [FF, D]), ("gwsl", [FF, D]),
                  ("uwsh", [FF, D]), ("uwsl", [FF, D]),
                  ("dwsh", [D, FF]), ("dwsl", [D, FF]),
                  ("dwfmh", [FC, 128, FF]), ("dwfml", [FC, 128, FF]),
                  ("hqfmh", [NJ, 128, FF]), ("hqfml", [NJ, 128, FF]),
                  ("dpscr", [2, 3, 128, 512])):
        T[n] = nc.dram_tensor(n, sh, BF16)
    for name, (shape, dt) in DEBUG_TAPS.items():
        T[name] = nc.dram_tensor(name, list(shape), dt,
                                 kind="ExternalOutput")

    p = Prog()
    es = contextlib.ExitStack()
    with es:
        sb = {}
        sb["a32"] = es.enter_context(nc.sbuf_tensor("a32", [128, 14336], F32))
        sb["a16"] = es.enter_context(nc.sbuf_tensor("a16", [128, 24576], BF16))
        sb["xqf"] = es.enter_context(nc.sbuf_tensor("xqf", [128, 32768], BF16))
        sb["cmsk"] = es.enter_context(nc.sbuf_tensor("cmsk", [128, 4 * QT], F32))
        sb["cbor"] = es.enter_context(nc.sbuf_tensor("cbor", [128, D], F32))
        sb["cgr"] = es.enter_context(nc.sbuf_tensor("cgr", [128, D], F32))
        sb["cbr"] = es.enter_context(nc.sbuf_tensor("cbr", [128, D], F32))
        sb["cbq"] = es.enter_context(nc.sbuf_tensor("cbq", [128, FC], F32))
        sb["cbk"] = es.enter_context(nc.sbuf_tensor("cbk", [128, FC], F32))
        sb["cbv"] = es.enter_context(nc.sbuf_tensor("cbv", [128, FC], F32))
        sb["coc"] = es.enter_context(nc.sbuf_tensor("coc", [128, 1], BF16))
        sb["cor"] = es.enter_context(nc.sbuf_tensor("cor", [1, 128], BF16))
        sb["stat"] = es.enter_context(nc.sbuf_tensor("stat", [128, 288], F32))
        sb["srow"] = es.enter_context(nc.sbuf_tensor("srow", [1, QT], F32))
        sb["srwb"] = es.enter_context(nc.sbuf_tensor("srwb", [1, 2 * QT], BF16))
        sb["q8t"] = es.enter_context(nc.sbuf_tensor("q8t", [128, 2048], FP8))
        sb["ps"] = [es.enter_context(nc.psum_tensor(f"ps{i}", [128, 512], F32))
                    for i in range(8)]
        sem_names = ("LD", "TT", "ST", "PE", "VE", "AC", "LW", "LR",
                     "L3", "L4", "LX")
        sems = {s: es.enter_context(nc.semaphore(s)) for s in sem_names}

        emit_all(p, nc, T, sb)
        p.check()

        blk = es.enter_context(nc.Block())
        handles = dict(pe=blk.tensor, ve=blk.vector, ac=blk.scalar,
                       sy=blk.sync, gp=blk.gpsimd)
        for e in ENGS:
            stream = p.streams[e]

            def replay(eng, stream=stream):
                last = {}
                for w, fn, inc, n, tag in stream:
                    for s, v in w:
                        if last.get(s, -1) < v:
                            eng.wait_ge(sems[s], v)
                            last[s] = v
                    ins = fn(eng)
                    if inc is not None:
                        ins.then_inc(sems[inc], n)
            handles[e](replay)
    return nc


def emit_all(p, nc, T, sb):
    a32, a16, xqf = sb["a32"], sb["a16"], sb["xqf"]
    ps = sb["ps"]
    stat, srow, srwb, q8t = sb["stat"], sb["srow"], sb["srwb"], sb["q8t"]
    ap = {n: h.ap() for n, h in T.items()}
    st = p.state

    def dma(eng, out, in_, sem, waits=(), tag=""):
        p.emit(eng, lambda e, o=out, i=in_: e.dma_start(out=o, in_=i),
               waits=waits, inc=sem, n=16, tag=tag)
        return p.tok(sem)

    def dmat(out, in_, waits=(), tag=""):
        p.emit("ac", lambda e, o=out, i=in_:
               e.dma_start(out=o, in_=i, transpose=True),
               waits=waits, inc="TT", n=16, tag=tag)
        return p.tok("TT")

    def ve(fn, waits=(), tag="", dr=True):
        if dr:
            p.emit("ve", fn, waits=waits, tag=tag)
            p.emit("ve", lambda e: e.drain(), inc="VE", tag=tag + ".d")
        else:
            p.emit("ve", fn, waits=waits, inc="VE", tag=tag)
        return p.tok("VE")

    def ac(fn, waits=(), tag="", dr=True):
        if dr:
            p.emit("ac", fn, waits=waits, tag=tag)
            p.emit("ac", lambda e: e.drain(), inc="AC", tag=tag + ".d")
        else:
            p.emit("ac", fn, waits=waits, inc="AC", tag=tag)
        return p.tok("AC")

    def pe(fn, waits=(), inc=None, tag=""):
        p.emit("pe", fn, waits=waits, inc=inc, tag=tag)
        return p.tok("PE")

    def tap(name, src, waits, tag="tap"):
        if name in DEBUG_TAPS:
            dma("gp", ap[name], src, "ST", waits, tag)

    dma("sy", sb["cmsk"][:, :], ap["mask4"], "LD")
    dma("sy", sb["cbor"][:, :], ap["bo2"].broadcast_to([128, D]), "LD")
    dma("sy", sb["cgr"][:, :], ap["lng"].broadcast_to([128, D]), "LD")
    dma("sy", sb["cbr"][:, :], ap["lnb"].broadcast_to([128, D]), "LD")
    dma("sy", sb["cbq"][:, :], ap["bq16"], "LD")
    dma("sy", sb["cbk"][:, :], ap["bk16"], "LD")
    dma("sy", sb["cbv"][:, :], ap["bv16"], "LD")
    dma("sy", sb["coc"][:, :], ap["onescol"], "LD")
    CONST = dma("sy", sb["cor"][:, :], ap["onesrow"], "LD")
    tap("t_cgr", sb["cgr"][:, :], [("LD", CONST)])

    def ln_chain(src_ap, dump_ap, out_ap, pre_waits, tag):
        mus, ngm, ve_, rec, rsq, ssq = (stat[:, i:i + 1] for i in range(6))
        w0 = list(pre_waits) + [("AC", st.get("ln_ac", 0))]
        ve(lambda e, i=src_ap, o=mus: e.reduce_sum(o, i, AX),
           waits=w0, tag=tag + ".sum")
        vt = ve(lambda e, o=ngm, i=mus: e.tensor_scalar_mul(o, i, -INV_D))
        at = ac(lambda e, i=src_ap, o=dump_ap, b=ngm, s=ssq:
                e.activation(o, i, AF.Square, bias=b, scale=1.0,
                             accum_out=s),
                waits=[("VE", vt), ("LD", CONST)], tag=tag + ".sq")
        ve(lambda e, o=ve_, i=ssq: e.tensor_scalar(o, i, INV_D, EPS,
                                                   ALU.mult, ALU.add),
           waits=[("AC", at)])
        vt = ve(lambda e, o=rec, i=ve_: e.reciprocal(o, i))
        at = ac(lambda e, o=rsq, i=rec: e.activation(o, i, AF.Sqrt),
                waits=[("VE", vt)])
        st["ln_ac"] = at
        r_ = ve(lambda e, o=out_ap, i=src_ap, s1=ngm, s2=rsq:
                e.tensor_scalar(o, i, s1, s2, ALU.add, ALU.mult),
                waits=[("AC", at)], tag=tag + ".nrm")
        tap("t_stat_" + tag, stat[:, :8], [("VE", r_)])
        return r_

    def mxq_chain(src_ap, out_ap, n, pre_waits, tag):
        nb = n // 32
        am = stat[:, 8:8 + nb]
        s_ = stat[:, 8 + nb:8 + 2 * nb]
        rs2 = stat[:, 8 + 2 * nb:8 + 3 * nb]
        s2 = stat[:, 8 + 3 * nb:8 + 4 * nb]
        q8 = q8t[:, :n]
        ve(lambda e, i=src_ap, o=am: e.tensor_reduce(
            o.rearrange("q (b o) -> q b o", o=1),
            i.rearrange("q (b i) -> q b i", i=32),
            AX, ALU.max, apply_absolute_value=True),
            waits=pre_waits, tag=tag + ".amax")
        ve(lambda e, o=s_, i=am: e.tensor_scalar(o, i, C448, EPS30,
                                                 ALU.mult, ALU.max))
        ve(lambda e, o=rs2, i=s_: e.reciprocal(o, i))
        ve(lambda e, o=rs2, i=rs2: e.tensor_scalar_mul(o, i, 0.5))
        ve(lambda e, o=s2, i=s_: e.tensor_scalar_mul(o, i, 2.0))
        ve(lambda e, o=q8, i=src_ap, s=rs2, b=nb: e.tensor_tensor(
            o.rearrange("q (k i) -> q k i", i=32),
            i.rearrange("q (k i) -> q k i", i=32),
            s.unsqueeze(2).broadcast_to([128, b, 32]), ALU.mult),
            tag=tag + ".q8")
        return ve(lambda e, o=out_ap, i=q8, s=s2, b=nb: e.tensor_tensor(
            o.rearrange("q (k i) -> q k i", i=32),
            i.rearrange("q (k i) -> q k i", i=32),
            s.unsqueeze(2).broadcast_to([128, b, 32]), ALU.mult),
            tag=tag + ".deq")

    XRAW, LNS, XN, WF, EV = 0, 4096, 6144, 8192, 12288
    XNH, XNL, WH, WL, EVH, EVL = 0, 2048, 4096, 8192, 12288, 14336
    pan_pe = 0
    for pan in range(4):
        tt_tok = {}
        for tl in range(8):
            row0 = pan * 1024 + tl * 128
            slot = tl % 2
            xr = a32[:, XRAW + slot * D: XRAW + slot * D + D]
            ldt = dma("sy", xr, ap["xb"][row0:row0 + 128, :], "LD",
                      [("VE", st.get(("xr", slot), 0))], f"x{pan}.{tl}")
            tap(f"t_xr_{pan}_{tl}", xr, [("LD", ldt)])
            nt = ln_chain(xr, a32[:, LNS:LNS + D], a32[:, XN:XN + D],
                          [("LD", ldt)], f"ln1.{pan}.{tl}")
            xn = a32[:, XN:XN + D]
            ve(lambda e, o=xn, i=xn, c=sb["cgr"][:, :]:
               e.tensor_tensor(o, i, c, ALU.mult))
            ve(lambda e, o=xn, i=xn, c=sb["cbr"][:, :]:
               e.tensor_tensor(o, i, c, ALU.add))
            xh = a16[:, XNH:XNH + D]
            xl = a16[:, XNL:XNL + D]
            ve(lambda e, o=xh, i=xn: e.tensor_copy(o, i),
               waits=[("ST", st.get("xnsp", 0))])
            vt = ve(lambda e, o=xl, a=xn, b=xh:
                    e.tensor_tensor(o, a, b, ALU.subtract))
            st[("xr", slot)] = vt
            j32 = pan * 8 + tl
            dma("gp", ap["xnscrh"][j32], xh, "ST", [("VE", vt)], f"spxn{j32}")
            xnsp = dma("gp", ap["xnscrl"][j32], xl, "ST")
            st["xnsp"] = xnsp
            if pan == 0 and tl == 0 and "t_pin_xn" in DEBUG_TAPS:
                dma("gp", ap["t_pin_xn"], xn, "ST", [("VE", vt)], "pinxn")
                dma("gp", ap["t_pin_xh"], xh, "ST", [("VE", vt)], "pinxh")
                dma("gp", ap["t_pin_st"], stat[:, :8], "ST", [("VE", vt)])
                pintok = p.tok("ST")
                p.pending["ve"].append(("ST", pintok))
                p.pending["ac"].append(("ST", pintok))
            tap(f"t_xn_{pan}_{tl}", xn, [("VE", vt)])
            for s, scrn in ((0, "xnscrh"), (1, "xnscrl")):
                dst = _ap(xqf, s * 16384 + tl * 128, [[1024, FC], [1, 128]])
                dmat(dst, ap[scrn][j32].rearrange("q (c f) -> q c f", f=128),
                     [("ST", xnsp), ("PE", pan_pe)], f"xnfT{pan}.{tl}.{s}")
            tt_tok[tl] = p.tok("TT")
        xnf_rdy = p.tok("TT")
        if pan == 0:
            tap("t_xnfh", xqf[:, :16384], [("TT", xnf_rdy)])
            tap("t_xnfl", xqf[:, 16384:], [("TT", xnf_rdy)])

        for wn, bias_t, dsth, dstl, scale, is_q in (
                ("wk", "cbk", "kscrh", "kscrl", None, False),
                ("wv", "cbv", "vscrh", "vscrl", None, False),
                ("wq", "cbq", "qscrh", "qscrl", RSQ, True)):
            for oc in range(FC):
                db = oc % 2
                wf = a32[:, WF + db * D: WF + db * D + D]
                wh = a16[:, WH + db * D: WH + db * D + D]
                wl = a16[:, WL + db * D: WL + db * D + D]
                wsrc = bass.AP(T[wn], oc * 128,
                               [[D, 128], [128 * D, FC], [1, 128]])
                lw = dma("sy", wf.rearrange("q (c f) -> q c f", f=128), wsrc,
                         "LW", [("VE", st.get(("wf", db), 0))], f"{wn}{oc}")
                ve(lambda e, o=wh, i=wf: e.tensor_copy(o, i),
                   waits=[("LW", lw), ("PE", st.get(("whl", db), 0))])
                wtok = ve(lambda e, o=wl, a=wf, b=wh:
                          e.tensor_tensor(o, a, b, ALU.subtract))
                st[("wf", db)] = wtok
                for hh in range(1 if is_q else 2):
                    bk = (oc * 2 + hh) % 4
                    bank = ps[bk][:, :]
                    for fc in range(FC):
                        if is_q:
                            rh = _ap(xqf, fc * 1024, [[256, 4], [1, 128]])
                            rl = _ap(xqf, 16384 + fc * 1024,
                                     [[256, 4], [1, 128]])
                        else:
                            off = fc * 1024 + hh * 512
                            rh = xqf[:, off:off + 512]
                            rl = xqf[:, 16384 + off:16384 + off + 512]
                        lh = wh[:, fc * 128:(fc + 1) * 128]
                        ll = wl[:, fc * 128:(fc + 1) * 128]
                        w0 = ([("VE", wtok), ("TT", xnf_rdy),
                               ("VE", st.get(("bank", bk), 0))]
                              if fc == 0 else ())
                        pe(lambda e, o=bank, a=lh, b=rh, s=(fc == 0):
                           e.matmul(o, a, b, start=s, stop=False), waits=w0)
                        pe(lambda e, o=bank, a=lh, b=rl:
                           e.matmul(o, a, b, start=False, stop=False))
                        petok = pe(lambda e, o=bank, a=ll, b=rh,
                                   q=(fc == FC - 1):
                                   e.matmul(o, a, b, start=False, stop=q),
                                   inc="PE" if fc == FC - 1 else None)
                    st[("whl", db)] = petok
                    eb = a32[:, EV + db * 512: EV + db * 512 + 512]
                    eh = a16[:, EVH + db * 512: EVH + db * 512 + 512]
                    el = a16[:, EVL + db * 512: EVL + db * 512 + 512]
                    bias = sb[bias_t][:, oc:oc + 1]
                    ww = [("PE", petok), ("ST", st.get(("ev", db), 0))]
                    if scale is None:
                        st[("bank", bk)] = ve(
                            lambda e, o=eb, i=bank, b=bias:
                            e.tensor_scalar(o, i, b, None, ALU.add),
                            waits=ww, tag=f"ev{wn}{oc}", dr=False)
                    else:
                        st[("bank", bk)] = ve(
                            lambda e, o=eb, i=bank, b=bias:
                            e.tensor_scalar(o, i, b, scale, ALU.add,
                                            ALU.mult),
                            waits=ww, tag=f"ev{wn}{oc}", dr=False)
                    ve(lambda e, o=eh, i=eb: e.tensor_copy(o, i))
                    vt = ve(lambda e, o=el, a=eb, b=eh:
                            e.tensor_tensor(o, a, b, ALU.subtract))
                    col0 = pan * 512 if is_q else pan * 1024 + hh * 512
                    for src, dn in ((eh, dsth), (el, dstl)):
                        dma("gp", ap[dn][oc][:, col0:col0 + 512], src, "ST",
                            [("VE", vt)], f"sp{wn}{oc}")
                    st[("ev", db)] = p.tok("ST")
        pan_pe = p.tok("PE")

    P1ST = p.tok("ST")
    RGH, RGL, SHL, AVHL, DWB = 8192, 9216, 10240, 10752, 11264
    RGF, SAC, BSB, AVN, DWF, DWD, WOF = 0, 1024, 1280, 1536, 2048, 6144, 8192

    def load_head(h):
        slot = h % 2
        base = slot * 16384
        w = [("PE", st.get(("kvq", slot), 0)), ("ST", P1ST)]
        dma("sy", xqf[:, base:base + 4096], ap["kscrh"][h], "LD", w, f"kh{h}")
        dma("sy", xqf[:, base + 4096:base + 8192], ap["kscrl"][h], "LD")
        dmat(xqf[:, base + 8192:base + 12288].rearrange(
            "q (c f) -> q c f", f=128),
             ap["vscrh"][h].rearrange("q (c f) -> q c f", f=128), w, f"vT{h}")
        dmat(xqf[:, base + 12288:base + 16384].rearrange(
            "q (c f) -> q c f", f=128),
             ap["vscrl"][h].rearrange("q (c f) -> q c f", f=128))
        dma("sy", a16[:, slot * 4096:slot * 4096 + 2048], ap["qscrh"][h],
            "LD", w, f"qh{h}")
        dma("sy", a16[:, slot * 4096 + 2048:slot * 4096 + 4096],
            ap["qscrl"][h], "LD")
        st[("kvqld", slot)] = (p.tok("LD"), p.tok("TT"))

    load_head(0)
    for h in range(NH):
        if h + 1 < NH:
            load_head(h + 1)
        slot = h % 2
        base = slot * 16384
        ldt, ttt = st[("kvqld", slot)]
        qh = a16[:, slot * 4096: slot * 4096 + 2048]
        ql = a16[:, slot * 4096 + 2048: slot * 4096 + 4096]
        for t in range(NQT):
            nkc = 4 * (t + 1)
            qhi = qh[:, t * QT:(t + 1) * QT]
            qlo = ql[:, t * QT:(t + 1) * QT]
            avb = 2 + (t % 2)
            av = ps[avb][:, :QT]
            def emit_av(kc, vt):
                r = kc % 4
                vh_ = xqf[:, base + 8192 + kc * 128:
                          base + 8192 + (kc + 1) * 128]
                vl_ = xqf[:, base + 12288 + kc * 128:
                          base + 12288 + (kc + 1) * 128]
                rh = a16[:, RGH + r * QT: RGH + (r + 1) * QT]
                rl = a16[:, RGL + r * QT: RGL + (r + 1) * QT]
                pe(lambda e, o=av, a=vh_, b=rh, s=(kc == 0):
                   e.matmul(o, a, b, start=s, stop=False),
                   waits=[("VE", vt)])
                pe(lambda e, o=av, a=vh_, b=rl:
                   e.matmul(o, a, b, start=False, stop=False))
                st[("ring", kc % 4)] = pe(
                    lambda e, o=av, a=vl_, b=rh, q=(kc == nkc - 1):
                    e.matmul(o, a, b, start=False, stop=q), inc="PE")

            pend = []
            for kc in range(nkc):
                sbk = (0, 1, 6, 7)[kc % 4]
                sbank = ps[sbk][:, :QT]
                r = kc % 4
                kh_ = xqf[:, base + kc * 128: base + (kc + 1) * 128]
                kl_ = xqf[:, base + 4096 + kc * 128: base + 4096 + (kc + 1) * 128]
                w0 = [("LD", ldt), ("TT", ttt),
                      ("AC", st.get(("sbank", sbk), 0))]
                pe(lambda e, o=sbank, a=kh_, b=qhi:
                   e.matmul(o, a, b, start=True, stop=False), waits=w0,
                   tag=f"sc{h}.{t}.{kc}")
                pe(lambda e, o=sbank, a=kh_, b=qlo:
                   e.matmul(o, a, b, start=False, stop=False))
                sct = pe(lambda e, o=sbank, a=kl_, b=qhi:
                         e.matmul(o, a, b, start=False, stop=True), inc="PE")
                aw = [("PE", sct)]
                if kc >= nkc - 4:
                    mi = kc - (nkc - 4)
                    vt = ve(lambda e, o=sbank,
                            m=sb["cmsk"][:, mi * QT:(mi + 1) * QT]:
                            e.tensor_tensor(o, o, m, ALU.add),
                            waits=[("PE", sct)], tag=f"msk{kc}")
                    aw = [("VE", vt)]
                rh = a16[:, RGH + r * QT: RGH + (r + 1) * QT]
                rl = a16[:, RGL + r * QT: RGL + (r + 1) * QT]
                rf = a32[:, RGF + r * QT: RGF + (r + 1) * QT]
                aw.append(("PE", st.get(("ring", r), 0)))
                ac(lambda e, o=rh, i=sbank: e.activation(o, i, AF.Exp),
                   waits=aw, tag=f"exp{h}.{t}.{kc}", dr=False)
                st[("sbank", sbk)] = ac(lambda e, o=rf, i=sbank:
                                        e.activation(o, i, AF.Exp), dr=False)
                atok = p.tok("AC")
                ve(lambda e, o=rl, a=rf, b=rh:
                   e.tensor_tensor(o, a, b, ALU.subtract),
                   waits=[("AC", atok)], dr=False)
                if kc == 0:
                    vt = ve(lambda e, o=a32[:, SAC:SAC + QT], i=rf:
                            e.tensor_copy(o, i), dr=False)
                else:
                    vt = ve(lambda e, o=a32[:, SAC:SAC + QT], i=rf:
                            e.tensor_tensor(o, o, i, ALU.add), dr=False)
                pend.append((kc, vt))
                if len(pend) > 2:
                    emit_av(*pend.pop(0))
            for item in pend:
                emit_av(*item)
            avpe = p.tok("PE")
            sh_ = a16[:, SHL:SHL + QT]
            sl_ = a16[:, SHL + QT:SHL + 2 * QT]
            ve(lambda e, o=sh_, i=a32[:, SAC:SAC + QT]: e.tensor_copy(o, i),
               waits=[("PE", st.get("shl_pe", 0))], dr=False)
            vt = ve(lambda e, o=sl_, a=a32[:, SAC:SAC + QT], b=sh_:
                    e.tensor_tensor(o, a, b, ALU.subtract))
            rowp = ps[4][:1, :QT]
            pe(lambda e, o=rowp, a=sb["coc"][:, :], b=sh_:
               e.matmul(o, a, b, start=True, stop=False),
               waits=[("VE", vt), ("VE", st.get("row_free", 0)),
                      ("LD", CONST)])
            rt = pe(lambda e, o=rowp, a=sb["coc"][:, :], b=sl_:
                    e.matmul(o, a, b, start=False, stop=True), inc="PE")
            st["shl_pe"] = rt
            vt = ve(lambda e, o=srow[:1, :QT], i=rowp: e.reciprocal(o, i),
                    waits=[("PE", rt)], tag=f"rcp{h}.{t}")
            st["row_free"] = vt
            ve(lambda e, o=srwb[:1, :QT], i=srow[:1, :QT]:
               e.tensor_copy(o, i), waits=[("PE", st.get("srwb_pe", 0))],
               dr=False)
            vt = ve(lambda e, o=srwb[:1, QT:2 * QT], a=srow[:1, :QT],
                    b=srwb[:1, :QT]: e.tensor_tensor(o, a, b, ALU.subtract))
            bp = ps[5][:, :QT]
            pe(lambda e, o=bp, a=sb["cor"][:1, :], b=srwb[:1, :QT]:
               e.matmul(o, a, b, start=True, stop=False),
               waits=[("VE", vt), ("VE", st.get("b_free", 0))])
            bt_ = pe(lambda e, o=bp, a=sb["cor"][:1, :],
                     b=srwb[:1, QT:2 * QT]:
                     e.matmul(o, a, b, start=False, stop=True), inc="PE")
            st["srwb_pe"] = bt_
            ve(lambda e, o=a32[:, BSB:BSB + QT], i=bp: e.tensor_copy(o, i),
               waits=[("PE", bt_)], dr=False)
            st["b_free"] = p.tok("VE")
            avn = a32[:, AVN:AVN + QT]
            vt = ve(lambda e, o=avn, a=av, b=a32[:, BSB:BSB + QT]:
                    e.tensor_tensor(o, a, b, ALU.mult),
                    waits=[("PE", avpe)], tag=f"avn{h}.{t}", dr=False)
            ah_ = a16[:, AVHL:AVHL + QT]
            al_ = a16[:, AVHL + QT:AVHL + 2 * QT]
            ve(lambda e, o=ah_, i=avn: e.tensor_copy(o, i),
               waits=[("ST", st.get("spav", 0))])
            vt = ve(lambda e, o=al_, a=avn, b=ah_:
                    e.tensor_tensor(o, a, b, ALU.subtract))
            tap(f"t_avn_{h}_{t}", avn, [("VE", vt)])
            dsth_ = bass.AP(T["ascrh"], t * (2 * 128 * NH * 128) + h * 128,
                            [[NH * 128, 128], [128 * NH * 128, 2], [1, 128]])
            dstl_ = bass.AP(T["ascrl"], t * (2 * 128 * NH * 128) + h * 128,
                            [[NH * 128, 128], [128 * NH * 128, 2], [1, 128]])
            dma("gp", dsth_, ah_.rearrange("q (c f) -> q c f", f=128), "ST",
                [("VE", vt)], f"spav{h}.{t}")
            st["spav"] = dma("gp", dstl_,
                             al_.rearrange("q (c f) -> q c f", f=128), "ST")
            if h == 0 and t == 0 and "t_p2vh" in DEBUG_TAPS:
                vt0 = p.tok("VE")
                dma("gp", ap["t_p2vh"], xqf[:, base + 8192:base + 12288],
                    "ST", [("VE", vt0)], "dbgvh")
                dma("gp", ap["t_p2rgf"], a32[:, RGF:RGF + 1024], "ST",
                    [("VE", vt0)])
                dma("gp", ap["t_p2sac"], a32[:, SAC:SAC + QT], "ST",
                    [("VE", vt0)])
                dma("gp", ap["t_p2bsb"], a32[:, BSB:BSB + QT], "ST",
                    [("VE", vt0)])
                dma("gp", ap["t_p2avn"], a32[:, AVN:AVN + QT], "ST",
                    [("VE", vt0)])
                dma("gp", ap["t_p2qh"], a16[:, slot * 4096:slot * 4096 + 2048],
                    "ST", [("VE", vt0)])
                st["p2dbg"] = p.tok("ST")
                p.pending["ve"].append(("ST", st["p2dbg"]))
                p.pending["ac"].append(("ST", st["p2dbg"]))
                p.pending["pe"].append(("ST", st["p2dbg"]))
        st[("kvq", slot)] = p.tok("PE")

        wof = a32[:, WOF:WOF + 2048]
        woh = a16[:, DWB:DWB + 2048]
        wol = a16[:, DWB + 2048:DWB + 4096]
        lw = dma("sy", wof, ap["wo"][h * 128:(h + 1) * 128, :], "LR",
                 [("VE", st.get("wof_ve", 0))], f"wo{h}")
        ve(lambda e, o=woh, i=wof: e.tensor_copy(o, i),
           waits=[("LW", 0), ("LR", lw), ("ST", st.get("wo_st", 0)),
                  ("TT", st.get("dwt", 0))])
        vt = ve(lambda e, o=wol, a=wof, b=woh:
                e.tensor_tensor(o, a, b, ALU.subtract))
        st["wof_ve"] = vt
        dma("gp", ap["woscrh"][h], woh, "ST", [("VE", vt)], f"spwo{h}")
        st["wo_st"] = dma("gp", ap["woscrl"][h], wol, "ST")
        for qtr in range(4):
            dwf = a32[:, DWF:DWF + 2048]
            dwd = a32[:, DWD:DWD + 2048]
            dh = a16[:, DWB + 4096:DWB + 6144]
            dl = a16[:, DWB + 6144:DWB + 8192]
            srcq = ap["dw"][h * 128:(h + 1) * 128, qtr * 2048:(qtr + 1) * 2048]
            lw = dma("sy", dwf, srcq, "LR", [("VE", st.get("dwf_ve", 0))],
                     f"dw{h}.{qtr}")
            st["dwf_ve"] = mxq_chain(dwf, dwd, 2048, [("LR", lw)],
                                     f"dwq{h}.{qtr}")
            ve(lambda e, o=dh, i=dwd: e.tensor_copy(o, i),
               waits=[("ST", st.get("dwsp", 0))])
            vt = ve(lambda e, o=dl, a=dwd, b=dh:
                    e.tensor_tensor(o, a, b, ALU.subtract))
            dma("gp", ap["dwsh"][h * 128:(h + 1) * 128,
                                 qtr * 2048:(qtr + 1) * 2048],
                dh, "ST", [("VE", vt)], f"spdw{h}")
            st["dwsp"] = dma("gp",
                             ap["dwsl"][h * 128:(h + 1) * 128,
                                        qtr * 2048:(qtr + 1) * 2048],
                             dl, "ST")
        for s_i, sn, fmn in ((0, "dwsh", "dwfmh"), (1, "dwsl", "dwfml")):
            strip = a16[:, DWB:DWB + 8192]
            tws = dmat(strip.rearrange("q (c f) -> q c f", f=128),
                       ap[sn][h * 128:(h + 1) * 128, :].rearrange(
                           "q (c f) -> q c f", f=128),
                       [("ST", st["dwsp"]), ("ST", st.get("dwfmsp", 0))],
                       f"dwfmT{h}.{s_i}")
            st["dwfmsp"] = dma("gp", ap[fmn][h], strip,
                               "ST", [("TT", tws)], f"dwfmsp{h}.{s_i}")

    st["p2_pe"] = p.tok("PE")
    emit_p34(p, nc, T, sb, ap, dma, dmat, ve, ac, pe, tap, ln_chain,
             mxq_chain, CONST, _ap)


def emit_p34(p, nc, T, sb, ap, dma, dmat, ve, ac, pe, tap, ln_chain,
             mxq_chain, CONST, _ap_):
    a32, a16, xqf = sb["a32"], sb["a16"], sb["xqf"]
    ps = sb["ps"]
    st = p.state
    p.barrier()

    OT, X1, LNS3, XQ = 0, 2048, 4096, 6144
    AVT, WOS, XQH = 0, 4096, 8192
    GWTH, GWTL, UWTH, UWTL, PRET, HQB, HQT = (0, 4096, 8192, 12288,
                                              16384, 20480, 21504)
    WFQ, WDQ, SIL, HB = 0, 2048, 4096, 4608

    st["xqf_pe"] = st.get("p2_pe", 0)
    for half in range(2):
        p.barrier()
        for tt in [0] + list(range(8)):
            j = half * 8 + tt
            w = [("PE", st.get("avt_pe", 0))]
            dma("sy", a16[:, AVT:AVT + 2048], ap["ascrh"][j // 2][j % 2],
                "L3", w, f"av{j}")
            lda = dma("sy", a16[:, AVT + 2048:AVT + 4096],
                      ap["ascrl"][j // 2][j % 2], "L3")
            otile = a32[:, OT:OT + D]
            for oc in range(4):
                bk = oc % 4
                bank = ps[bk][:, :]
                for fc in range(FC):
                    db = fc % 2
                    rh = a16[:, WOS + db * 1024: WOS + db * 1024 + 512]
                    rl = a16[:, WOS + db * 1024 + 512:
                             WOS + db * 1024 + 1024]
                    dma("sy", rh,
                        ap["woscrh"][fc][:, oc * 512:(oc + 1) * 512],
                        "LR", [("PE", st.get(("wos", db), 0))],
                        f"wo{j}.{oc}.{fc}")
                    lw = dma("sy", rl,
                             ap["woscrl"][fc][:, oc * 512:(oc + 1) * 512],
                             "LR")
                    lh = a16[:, AVT + fc * 128: AVT + (fc + 1) * 128]
                    ll = a16[:, AVT + 2048 + fc * 128:
                             AVT + 2048 + (fc + 1) * 128]
                    w0 = [("LR", lw), ("L3", lda)]
                    if fc == 0:
                        w0.append(("VE", st.get(("p3bank", bk), 0)))
                    pe(lambda e, o=bank, a=lh, b=rh, s=(fc == 0):
                       e.matmul(o, a, b, start=s, stop=False), waits=w0)
                    pe(lambda e, o=bank, a=lh, b=rl:
                       e.matmul(o, a, b, start=False, stop=False))
                    petok = pe(lambda e, o=bank, a=ll, b=rh,
                               q=(fc == FC - 1):
                               e.matmul(o, a, b, start=False, stop=q),
                               inc="PE")
                    st[("wos", db)] = petok
                st[("p3bank", bk)] = ve(
                    lambda e, o=otile[:, oc * 512:(oc + 1) * 512], i=bank:
                    e.tensor_copy(o, i), waits=[("PE", petok)],
                    tag=f"oev{j}.{oc}", dr=False)
            st["avt_pe"] = petok
            vt = ve(lambda e, o=otile, i=otile, c=sb["cbor"][:, :]:
                    e.tensor_tensor(o, i, c, ALU.add), dr=False)
            ldx = dma("sy", a32[:, LNS3:LNS3 + D],
                      ap["xb"][j * 256: j * 256 + 128, :], "LX",
                      [("AC", st.get("ln_ac", 0))], f"xo{j}")
            vt = ve(lambda e, o=otile, a=otile, b=a32[:, LNS3:LNS3 + D]:
                    e.tensor_tensor(o, a, b, ALU.add), waits=[("LX", ldx)])
            x1 = a32[:, X1:X1 + D]
            vt = ln_chain(otile, a32[:, LNS3:LNS3 + D], x1,
                          [("VE", vt), ("ST", st.get("x1st", 0))],
                          f"ln2.{j}")
            tap(f"t_x1_{j}", x1, [("VE", vt)])
            st["x1st"] = dma("gp", ap["x1scr"][j], x1, "ST", [("VE", vt)],
                             f"spx1{j}")
            xq = a32[:, XQ:XQ + D]
            vt = mxq_chain(x1, xq, D, [("VE", vt)], f"xq{j}")
            xh = a16[:, XQH:XQH + D]
            xl = a16[:, XQH + D:XQH + 2 * D]
            ve(lambda e, o=xh, i=xq: e.tensor_copy(o, i),
               waits=[("ST", st.get("xqsp", 0))])
            vt = ve(lambda e, o=xl, a=xq, b=xh:
                    e.tensor_tensor(o, a, b, ALU.subtract))
            dma("gp", ap["xqscrh"][j], xh, "ST", [("VE", vt)], f"spxq{j}")
            xqsp = dma("gp", ap["xqscrl"][j], xl, "ST")
            st["xqsp"] = xqsp
            for s, scrn in ((0, "xqscrh"), (1, "xqscrl")):
                dst = _ap_(xqf, s * 16384 + tt * 128, [[1024, FC], [1, 128]])
                dmat(dst, ap[scrn][j].rearrange("q (c f) -> q c f", f=128),
                     [("ST", xqsp), ("PE", st["xqf_pe"])], f"xqT{j}.{s}")
            st["xqT"] = p.tok("TT")
        xqf_rdy = p.tok("TT")

        p.barrier()
        for wg in range(32):
            tts = 0
            for wname, bh, bl, sh_n, sl_n in (
                    ("gw", GWTH, GWTL, "gwsh", "gwsl"),
                    ("uw", UWTH, UWTL, "uwsh", "uwsl")):
                for m in range(2):
                    r0 = wg * 256 + m * 128
                    if half == 0:
                        wfq = a32[:, WFQ:WFQ + D]
                        wdq = a32[:, WDQ:WDQ + D]
                        ph_ = a16[:, PRET:PRET + D]
                        pl_ = a16[:, PRET + D:PRET + 2 * D]
                        lw = dma("sy", wfq, ap[wname][r0:r0 + 128, :], "L4",
                                 [("VE", st.get("wfq_ve", 0))],
                                 f"{wname}{wg}.{m}")
                        st["wfq_ve"] = mxq_chain(wfq, wdq, D, [("L4", lw)],
                                                 f"{wname}q{wg}.{m}")
                        ve(lambda e, o=ph_, i=wdq: e.tensor_copy(o, i),
                           waits=[("ST", st.get("wssp", 0))])
                        vt = ve(lambda e, o=pl_, a=wdq, b=ph_:
                                e.tensor_tensor(o, a, b, ALU.subtract))
                        dma("gp", ap[sh_n][r0:r0 + 128, :], ph_, "ST",
                            [("VE", vt)], f"spw{wname}{wg}.{m}")
                        wssp = dma("gp", ap[sl_n][r0:r0 + 128, :], pl_, "ST")
                        st["wssp"] = wssp
                    else:
                        wssp = 0
                    for s_i, scrn, bbase in ((0, sh_n, bh), (1, sl_n, bl)):
                        srcap = bass.AP(T[scrn], r0 * D,
                                        [[D, 128], [128, FC], [1, 128]])
                        tts = dmat(
                            _ap_(a16, bbase + m * 128, [[256, FC], [1, 128]]),
                            srcap,
                            [("ST", wssp), ("PE", st.get("gu_pe", 0))],
                            f"{wname}T{wg}.{m}")
                    st["pretT"] = tts
            wrdy = tts
            for tt in range(8):
                gb, ub = ps[(tt % 2) * 2], ps[(tt % 2) * 2 + 1]
                gps, ups = gb[:, :256], ub[:, :256]
                for fc in range(FC):
                    lh = xqf[:, fc * 1024 + tt * 128:
                             fc * 1024 + (tt + 1) * 128]
                    ll = xqf[:, 16384 + fc * 1024 + tt * 128:
                             16384 + fc * 1024 + (tt + 1) * 128]
                    grh = a16[:, GWTH + fc * 256: GWTH + (fc + 1) * 256]
                    grl = a16[:, GWTL + fc * 256: GWTL + (fc + 1) * 256]
                    urh = a16[:, UWTH + fc * 256: UWTH + (fc + 1) * 256]
                    url = a16[:, UWTL + fc * 256: UWTL + (fc + 1) * 256]
                    w0 = []
                    if fc == 0:
                        w0 = [("TT", wrdy), ("TT", xqf_rdy),
                              ("VE", st.get(("gub", tt % 2), 0))]
                    pe(lambda e, o=gps, a=lh, b=grh, s=(fc == 0):
                       e.matmul(o, a, b, start=s, stop=False), waits=w0)
                    pe(lambda e, o=gps, a=lh, b=grl:
                       e.matmul(o, a, b, start=False, stop=False))
                    pe(lambda e, o=gps, a=ll, b=grh:
                       e.matmul(o, a, b, start=False, stop=(fc == FC - 1)))
                    pe(lambda e, o=ups, a=lh, b=urh, s=(fc == 0):
                       e.matmul(o, a, b, start=s, stop=False))
                    pe(lambda e, o=ups, a=lh, b=url:
                       e.matmul(o, a, b, start=False, stop=False))
                    petok = pe(lambda e, o=ups, a=ll, b=urh,
                               q=(fc == FC - 1):
                               e.matmul(o, a, b, start=False, stop=q),
                               inc="PE" if fc == FC - 1 else None)
                st["gu_pe"] = petok
                st["xqf_pe"] = petok
                sil = a32[:, SIL:SIL + 256]
                at = ac(lambda e, o=sil, i=gps: e.activation(o, i, AF.Silu),
                        waits=[("PE", petok), ("VE", st.get("hbm_ve", 0))],
                        tag=f"silu{wg}.{tt}", dr=False)
                hb = a32[:, HB:HB + 256]
                vt = ve(lambda e, o=hb, a=sil, b=ups:
                        e.tensor_tensor(o, a, b, ALU.mult),
                        waits=[("AC", at)], dr=False)
                st[("gub", tt % 2)] = vt
                st["hbm_ve"] = vt
                if half == 0 and wg == 0 and tt == 0 and "t_p4sil" in DEBUG_TAPS:
                    dma("gp", ap["t_p4sil"], sil, "ST", [("VE", vt)], "p4sil")
                    dma("gp", ap["t_p4hb"], hb, "ST", [("VE", vt)])
                    dma("gp", ap["t_p4gwt"], a16[:, GWTH:GWTH + 4096], "ST",
                        [("VE", vt)])
                    dma("gp", ap["t_p4xqf"], xqf[:, :4096], "ST",
                        [("VE", vt)])
                    ptk = p.tok("ST")
                    for een in ("ve", "ac", "pe"):
                        p.pending[een].append(("ST", ptk))
                hq = a32[:, WDQ + D:WDQ + D + 256] if False else \
                    a32[:, SIL + 256:SIL + 512]
                hq = a32[:, HB + 256:HB + 512]
                vt = mxq_chain(hb, hq, 256, [("VE", vt)], f"hq{wg}.{tt}")
                hh_ = a16[:, HQB:HQB + 256]
                hl_ = a16[:, HQB + 256:HQB + 512]
                ve(lambda e, o=hh_, i=hq: e.tensor_copy(o, i),
                   waits=[("ST", st.get("hq_st", 0))])
                vt = ve(lambda e, o=hl_, a=hq, b=hh_:
                        e.tensor_tensor(o, a, b, ALU.subtract))
                jj = half * 8 + tt
                dma("gp", ap["hqscrh"][jj][:, wg * 256:(wg + 1) * 256],
                    hh_, "ST", [("VE", vt)], f"sphq{jj}.{wg}")
                st["hq_st"] = dma(
                    "gp", ap["hqscrl"][jj][:, wg * 256:(wg + 1) * 256],
                    hl_, "ST")

        gu_done = st.get("gu_pe", 0)
        for tt in range(8):
            jj = half * 8 + tt
            for s_i, sn, fmn in ((0, "hqscrh", "hqfmh"), (1, "hqscrl", "hqfml")):
                strip = a16[:, GWTH:GWTH + 8192]
                tws = dmat(strip.rearrange("q (c f) -> q c f", f=128),
                           ap[sn][jj].rearrange("q (c f) -> q c f", f=128),
                           [("ST", st.get("hq_st", 0)), ("PE", gu_done),
                            ("ST", st.get("hqfmsp", 0))],
                           f"hqfmT{jj}.{s_i}")
                st["hqfmsp"] = dma("gp", ap[fmn][jj], strip,
                                   "ST", [("TT", tws)], f"hqfmsp{jj}.{s_i}")

    emit_down(p, nc, T, sb, ap, dma, dmat, ve, ac, pe, tap, ln_chain,
              CONST, _ap_)


def emit_down(p, nc, T, sb, ap, dma, dmat, ve, ac, pe, tap, ln_chain,
              CONST, _ap_):
    a32, a16 = sb["a32"], sb["a16"]
    ps = sb["ps"]
    st = p.state
    p.barrier()
    LHS, RHS, PCS, TOC = 0, 4096, 8192, 9728
    DT, R1, X1L, LNS4 = 0, 8192, 8704, 10752

    for tc in range(4):
        for ocg in range(2):
            for hc in range(HFC):
                dbl = hc % 2
                lhs = a16[:, LHS + dbl * 2048: LHS + dbl * 2048 + 2048]
                rhs_h = a16[:, RHS + dbl * 1024: RHS + dbl * 1024 + 512]
                rhs_l = a16[:, RHS + dbl * 1024 + 512:
                            RHS + dbl * 1024 + 1024]
                wld = [("PE", st.get(("dslot", dbl), 0))]
                dma("sy", lhs[:, :1024].rearrange("q (c f) -> q c f", f=128),
                    bass.AP(T["dwfmh"], ocg * 8 * 128 * FF + hc * 128,
                            [[FF, 128], [128 * FF, 8], [1, 128]]),
                    "L4", wld, f"dlh{tc}.{ocg}.{hc}")
                dma("sy", lhs[:, 1024:2048].rearrange("q (c f) -> q c f", f=128),
                    bass.AP(T["dwfml"], ocg * 8 * 128 * FF + hc * 128,
                            [[FF, 128], [128 * FF, 8], [1, 128]]), "L4")
                dma("sy", rhs_h.rearrange("q (c f) -> q c f", f=128),
                    bass.AP(T["hqfmh"], tc * 4 * 128 * FF + hc * 128,
                            [[FF, 128], [128 * FF, 4], [1, 128]]),
                    "L4", wld, f"drh{tc}.{ocg}.{hc}")
                lw = dma("sy", rhs_l.rearrange("q (c f) -> q c f", f=128),
                         bass.AP(T["hqfml"], tc * 4 * 128 * FF + hc * 128,
                                 [[FF, 128], [128 * FF, 4], [1, 128]]), "L4")
                for oi in range(8):
                    bank = ps[oi][:, :]
                    lh = lhs[:, oi * 128:(oi + 1) * 128]
                    ll = lhs[:, 1024 + oi * 128: 1024 + (oi + 1) * 128]
                    w0 = [("L4", lw)]
                    if hc == 0:
                        w0.append(("VE", st.get(("dbank", oi), 0)))
                    pe(lambda e, o=bank, a=lh, b=rhs_h, s=(hc == 0):
                       e.matmul(o, a, b, start=s, stop=False), waits=w0,
                       tag=f"dn{tc}.{ocg}.{hc}.{oi}")
                    pe(lambda e, o=bank, a=lh, b=rhs_l:
                       e.matmul(o, a, b, start=False, stop=False))
                    petok = pe(lambda e, o=bank, a=ll, b=rhs_h,
                               q=(hc == HFC - 1):
                               e.matmul(o, a, b, start=False, stop=q),
                               inc="PE" if oi == 7 else None)
                st[("dslot", dbl)] = petok
            grp_pe = p.tok("PE")
            for oi in range(8):
                oc = ocg * 8 + oi
                slot = oc % 2
                bank = ps[oi][:, :]
                hi = a16[:, PCS:PCS + 512]
                r1 = a32[:, R1:R1 + 512]
                lo = a16[:, PCS + 512:PCS + 1024]
                lo2 = a16[:, PCS + 1024:PCS + 1536]
                ve(lambda e, o=hi, i=bank: e.tensor_copy(o, i),
                   waits=[("PE", grp_pe), ("ST", st.get("dpsp", 0))],
                   tag=f"d3w{tc}.{oc}")
                ve(lambda e, o=r1, a=bank, b=hi:
                   e.tensor_tensor(o, a, b, ALU.subtract))
                ve(lambda e, o=lo, i=r1: e.tensor_copy(o, i))
                vt = ve(lambda e, o=lo2, a=r1, b=lo:
                        e.tensor_tensor(o, a, b, ALU.subtract))
                st[("dbank", oi)] = vt
                spw = [("VE", vt), ("TT", st.get(("dpT", slot), 0))]
                for k, srcp in enumerate((hi, lo, lo2)):
                    dma("gp", ap["dpscr"][slot][k], srcp, "ST",
                        spw if k == 0 else (), f"spdp{tc}.{oc}.{k}")
                for k, srcp in enumerate((hi, lo, lo2)):
                    dma("gp", ap["dpscr"][slot][k], srcp, "ST", (),
                        f"spdp2{tc}.{oc}.{k}")
                st["dpsp"] = p.tok("ST")
                ttok = 0
                for rep_ in range(2):
                    for k in range(3):
                        ttok = dmat(
                            a16[:, TOC + k * 512:TOC + (k + 1) * 512].rearrange(
                                "q (c f) -> q c f", f=128),
                            ap["dpscr"][slot][k].rearrange(
                                "q (c f) -> q c f", f=128),
                            ([("ST", st["dpsp"]),
                              ("VE", st.get("dacc", 0))]
                             if (k == 0 and rep_ == 0) else ()),
                            f"dT{tc}.{oc}.{k}.{rep_}")
                st[("dpT", slot)] = ttok
                for k in range(3):
                    srcp = _ap_(a16, TOC + k * 512, [[128, 4], [1, 128]])
                    dstp = _ap_(a32, DT + oc * 128, [[2048, 4], [1, 128]])
                    if k == 0:
                        st["dacc"] = ve(
                            lambda e, o=dstp, i=srcp: e.tensor_copy(o, i),
                            waits=[("TT", ttok)], tag=f"dacc{tc}.{oc}")
                    else:
                        st["dacc"] = ve(
                            lambda e, o=dstp, a=dstp, b=srcp:
                            e.tensor_tensor(o, a, b, ALU.add),
                            waits=[("TT", ttok)], dr=False)
        acc_done = p.tok("VE")
        for tt4 in range(4):
            j = tc * 4 + tt4
            dtt = a32[:, DT + tt4 * 2048: DT + (tt4 + 1) * 2048]
            dma("sy", a32[:, X1L:X1L + D], ap["x1scr"][j], "L4",
                [("ST", st.get("yst", 0))], f"x1l{j}")
            ldx = dma("sy", a32[:, X1L:X1L + D], ap["x1scr"][j], "L4",
                      (), f"x1lb{j}")
            vt = ve(lambda e, o=dtt, a=dtt, b=a32[:, X1L:X1L + D]:
                    e.tensor_tensor(o, a, b, ALU.add),
                    waits=[("L4", ldx), ("VE", acc_done)])
            vt = ln_chain(dtt, a32[:, LNS4:LNS4 + D], a32[:, LNS4:LNS4 + D],
                          [("VE", vt), ("ST", st.get("yst", 0))], f"ln3.{j}")
            tap(f"t_y_{j}", a32[:, LNS4:LNS4 + D], [("VE", vt)])
            st["yst"] = dma("gp", ap["y"][j * 128:(j + 1) * 128, :],
                            a32[:, LNS4:LNS4 + D], "ST", [("VE", vt)],
                            f"y{j}")
    for name in DEBUG_TAPS:
        if name.startswith("d_"):
            _, scr, idx = name.split("_")
            dma("gp", ap[name], ap[scr][int(idx)], "ST",
                [("ST", p.tok("ST"))], f"dump{name}")
    p.emit("gp", lambda e: e.memset(a16[:1, :1], 0),
           waits=[("ST", p.tok("ST"))], tag="final")


def kernel(L0_attn_X, L0_attn_ln_g, L0_attn_ln_b, L0_attn_Wq, L0_attn_bq,
           L0_attn_Wk, L0_attn_bk, L0_attn_Wv, L0_attn_bv, L0_attn_Wo,
           L0_attn_bo, L0_ffn_gate_w, L0_ffn_up_w, L0_ffn_down_w):
    X = np.ascontiguousarray(np.asarray(L0_attn_X, dtype=np.float32))
    nc = build_nc()

    def b16(v):
        return np.ascontiguousarray(np.asarray(v, dtype=np.float32)
                                    .reshape(FC, 128).T)

    common = {
        "wq": np.ascontiguousarray(np.asarray(L0_attn_Wq, np.float32)),
        "wk": np.ascontiguousarray(np.asarray(L0_attn_Wk, np.float32)),
        "wv": np.ascontiguousarray(np.asarray(L0_attn_Wv, np.float32)),
        "wo": np.ascontiguousarray(np.asarray(L0_attn_Wo, np.float32)),
        "gw": np.ascontiguousarray(np.asarray(L0_ffn_gate_w, np.float32)),
        "uw": np.ascontiguousarray(np.asarray(L0_ffn_up_w, np.float32)),
        "dw": np.ascontiguousarray(np.asarray(L0_ffn_down_w, np.float32)),
        "lng": np.ascontiguousarray(np.asarray(L0_attn_ln_g, np.float32)
                                    .reshape(1, D)),
        "lnb": np.ascontiguousarray(np.asarray(L0_attn_ln_b, np.float32)
                                    .reshape(1, D)),
        "bo2": np.ascontiguousarray(np.asarray(L0_attn_bo, np.float32)
                                    .reshape(1, D)),
        "bq16": b16(L0_attn_bq), "bk16": b16(L0_attn_bk),
        "bv16": b16(L0_attn_bv),
        "onescol": np.ones((128, 1), dtype=ml_dtypes.bfloat16),
        "onesrow": np.ones((1, 128), dtype=ml_dtypes.bfloat16),
    }
    masks = {}
    for par in range(2):
        m = np.full((128, 4, 2, 128), NEG, dtype=np.float32)
        k = np.arange(128)[:, None]
        r = np.arange(128)[None, :]
        for i in range(4):
            for cq in range(2):
                if i < 2 * cq:
                    m[:, i, cq, :] = 0.0
                elif i == 2 * cq:
                    m[:, i, cq, :] = np.where(k <= r, 0.0, NEG)
                elif i == 2 * cq + 1 and par == 1:
                    m[:, i, cq, :] = 0.0
        masks[par] = np.ascontiguousarray(m.reshape(128, 4 * QT))

    in_maps = []
    for c in range(8):
        b, par = c // 2, c % 2
        xbm = X[b]
        if par == 1:
            perm = [i + 1 if i % 2 == 0 else i - 1 for i in range(32)]
            xbm = xbm.reshape(32, 128, D)[perm].reshape(S, D)
        m = dict(common)
        m["xb"] = np.ascontiguousarray(xbm)
        m["mask4"] = masks[par]
        in_maps.append(m)

    res = run_bass_kernel_spmd(nc, in_maps, core_ids=list(range(8)))
    out = np.empty((B, S, D), dtype=np.float32)
    for c in range(8):
        b, par = c // 2, c % 2
        yc = np.asarray(res.results[c]["y"]).reshape(NJ, 128, D)
        for j in range(NJ):
            out[b, (2 * j + par) * 128:(2 * j + par + 1) * 128, :] = yc[j]
    return out
